# version 17
# baseline (speedup 1.0000x reference)
import math
import sys
import contextlib
import numpy as np
import ml_dtypes

import concourse.bass as bass
import concourse.mybir as mybir
from concourse.bass_utils import run_bass_kernel_spmd

F32 = mybir.dt.float32
BF16 = mybir.dt.bfloat16
F8 = mybir.dt.float8e4
ALU = mybir.AluOpType
ACTF = mybir.ActivationFunctionType
AX = mybir.AxisListType

P = 128
D = 2048
HID = 8192
NT = 16
MT = 8
MY = MT * P
INV_SQRT_DK0 = 1.0 / math.sqrt(128.0)
INV_SQRT_HD1 = 1.0 / 32.0
TAU_F = 0.5 * (1.0 + 2.0 ** -9)
BF16_1EM8 = float(ml_dtypes.bfloat16(1e-8))
RMS_EPS = float(np.finfo(np.float32).eps)
LN_EPS = 1e-5
_SEC_LIMIT = None
_TERN_LIMIT = None
_DUMP = None

SC_PIECES = []
for _kp in range(16):
    _nu = (8 - _kp) if _kp < 8 else (16 - _kp)
    _qoff = 1024 - _nu * P
    _pieces = []
    _c = _qoff
    while _c < 1024:
        _c2 = min(1024, (_c // 512 + 1) * 512)
        _pieces.append((_c, _c2))
        _c = _c2
    SC_PIECES.append((_qoff, _pieces))
FULL_PIECES = [(0, 512), (512, 1024)]


class Ctr:
    def __init__(self, handle):
        self.h = handle
        self.n = 0


class _MockInstr:
    def __init__(self, eng=None):
        self._eng = eng

    def then_inc(self, h, k):
        if self._eng is not None:
            self._eng._attach_inc(h, k)
        return self


class _MockEng:

    def __init__(self):
        self.trace = {}
        self._engine = None

    def _items(self):
        return self.trace.setdefault(self._engine, [])

    def _attach_inc(self, h, k):
        items = self._items()
        if items and items[-1][0] == "instr":
            items[-1][2].append((id(h), k))
        else:
            items.append(["instr", "nop", [(id(h), k)], 0])

    def __getattr__(self, name):
        eng = self
        if name == "wait_ge":
            def w(h, v):
                try:
                    lines = (sys._getframe(1).f_lineno,
                             sys._getframe(2).f_lineno)
                except ValueError:
                    lines = (sys._getframe(1).f_lineno,)
                eng._items().append(["wait", id(h), v, lines])
                return _MockInstr(eng)
            return w
        if name == "sem_inc":
            def si(h, k):
                eng._items().append(["instr", "sem_inc", [(id(h), k)],
                                     sys._getframe(1).f_lineno])
                return _MockInstr(eng)
            return si
        if name == "then_inc":
            def ti(h, k):
                eng._attach_inc(h, k)
                return _MockInstr(eng)
            return ti

        def call(*a, **kw):
            eng._items().append(["instr", name, [],
                                 sys._getframe(1).f_lineno])
            return _MockInstr(eng)
        return call


class _EngProxy:

    def __init__(self, real):
        object.__setattr__(self, "_real", real)
        object.__setattr__(self, "_last", None)

    def __getattr__(self, name):
        attr = getattr(self._real, name)
        if callable(attr):
            def call(*a, **kw):
                r = attr(*a, **kw)
                if name not in ("wait_ge", "drain") and hasattr(r, "then_inc"):
                    object.__setattr__(self, "_last", r)
                return r
            return call
        return attr


def _simulate_queues(trace):
    sems = {}
    heads = {e: 0 for e in trace}
    progress = True
    while progress:
        progress = False
        for e, items in trace.items():
            i = heads[e]
            while i < len(items):
                it = items[i]
                if it[0] == "wait" and sems.get(it[1], 0) < it[2]:
                    break
                if it[0] == "instr":
                    for h, k in it[2]:
                        sems[h] = sems.get(h, 0) + k
                i += 1
                progress = True
            heads[e] = i
    stuck = {e: (heads[e], trace[e][heads[e]])
             for e in trace if heads[e] < len(trace[e])}
    return stuck, sems


class Emitter:

    def __init__(self, sems):
        self.sems = sems
        self.marks = {}
        self.mark_ctr = {}
        self.mock = True

    def reset(self):
        for c in self.sems.values():
            c.n = 0

    junk = None

    def inc(self, obj, ctr, k=1):
        if isinstance(obj, _EngProxy):
            tgt = obj._last
            if tgt is None or getattr(obj, "_inced", None) is tgt:
                nm = type(obj._real).__name__.lower()
                if "vector" in nm or "dve" in nm:
                    obj.tensor_copy(self.junk[:, 0:1], self.junk[:, 1:2])
                elif "scalar" in nm or "act" in nm:
                    obj.activation(self.junk[:, 0:1], self.junk[:, 1:2],
                                   ACTF.Copy)
                else:
                    raise RuntimeError(f"double inc on engine {nm}")
                tgt = obj._last
            tgt.then_inc(ctr.h, k)
            object.__setattr__(obj, "_inced", tgt)
        elif hasattr(obj, "then_inc"):
            obj.then_inc(ctr.h, k)
        else:
            raise TypeError(f"E.inc target {obj!r} has no then_inc")
        ctr.n += k

    def mark(self, key, ctr):
        if self.mock:
            self.marks[key] = ctr.n
            self.mark_ctr[key] = id(ctr)
        else:
            assert self.marks[key] == ctr.n, (key, self.marks[key], ctr.n)

    def wait(self, eng, ctr, key):
        if key not in self.marks and not self.mock:
            if _SEC_LIMIT is not None or _TERN_LIMIT is not None:
                return
            raise AssertionError(f"wait on unrecorded mark {key}")
        if key in self.mark_ctr and self.mark_ctr[key] != id(ctr):
            raise AssertionError(f"wait on mark {key} with wrong semaphore")
        v = self.marks.get(key, 0)
        if v > 0:
            eng.wait_ge(ctr.h, v)


def build_nc():
    nc = bass.Bass("TRN2", target_bir_lowering=False)

    x0 = nc.dram_tensor("x0", [NT * P, D], F32, kind="ExternalInput").ap()
    cmask = nc.dram_tensor("cmask", [16, P, P], BF16, kind="ExternalInput").ap()
    wq0m = nc.dram_tensor("wq0m", [256, D], F32, kind="ExternalInput").ap()
    wk0m = nc.dram_tensor("wk0m", [256, D], F32, kind="ExternalInput").ap()
    wv0m = nc.dram_tensor("wv0m", [256, D], F32, kind="ExternalInput").ap()
    wo0m = nc.dram_tensor("wo0m", [256, D], F32, kind="ExternalInput").ap()
    wfcm = nc.dram_tensor("wfcm", [HID // 8, D], F32, kind="ExternalInput").ap()
    wprm = nc.dram_tensor("wprm", [D // 8, HID], F32, kind="ExternalInput").ap()
    wq1 = nc.dram_tensor("wq1", [D, D], BF16, kind="ExternalInput").ap()
    wk1 = nc.dram_tensor("wk1", [D, D], BF16, kind="ExternalInput").ap()
    wv1 = nc.dram_tensor("wv1", [D, D], BF16, kind="ExternalInput").ap()
    wo1 = nc.dram_tensor("wo1", [D, D], BF16, kind="ExternalInput").ap()
    bq1 = nc.dram_tensor("bq1", [P, 16], BF16, kind="ExternalInput").ap()
    bk1 = nc.dram_tensor("bk1", [P, 16], BF16, kind="ExternalInput").ap()
    bv1 = nc.dram_tensor("bv1", [P, 16], BF16, kind="ExternalInput").ap()
    bo1 = nc.dram_tensor("bo1", [P, 16], BF16, kind="ExternalInput").ap()
    wf1m = nc.dram_tensor("wf1m", [256, D], F32, kind="ExternalInput").ap()
    out = nc.dram_tensor("out", [MY, D], F32, kind="ExternalOutput").ap()

    wq0i = nc.dram_tensor("wq0i", [256, D], BF16, kind="Internal").ap()
    wk0i = nc.dram_tensor("wk0i", [256, D], BF16, kind="Internal").ap()
    wv0i = nc.dram_tensor("wv0i", [256, D], BF16, kind="Internal").ap()
    wo0i = nc.dram_tensor("wo0i", [256, D], BF16, kind="Internal").ap()
    wq0f = nc.dram_tensor("wq0f", [D, D], BF16, kind="Internal").ap()
    wk0f = nc.dram_tensor("wk0f", [D, D], BF16, kind="Internal").ap()
    wv0f = nc.dram_tensor("wv0f", [D, D], BF16, kind="Internal").ap()
    wo0f = nc.dram_tensor("wo0f", [D, D], BF16, kind="Internal").ap()
    w1ti = nc.dram_tensor("w1ti", [D, HID // 8], BF16, kind="Internal").ap()
    w2ti = nc.dram_tensor("w2ti", [HID, D // 8], BF16, kind="Internal").ap()
    wd1i = nc.dram_tensor("wd1i", [256, D], BF16, kind="Internal").ap()
    w1tf = nc.dram_tensor("w1tf", [8 * D, HID // 8], BF16, kind="Internal").ap()
    w2tf = nc.dram_tensor("w2tf", [8 * HID, D // 8], BF16, kind="Internal").ap()
    wd1f = nc.dram_tensor("wd1f", [D, D], BF16, kind="Internal").ap()
    ln2i = nc.dram_tensor("ln2i", [D, MY], BF16, kind="Internal").ap()
    ln2f = nc.dram_tensor("ln2f", [2 * D, MY], BF16, kind="Internal").ap()
    v1fm = nc.dram_tensor("v1fm", [D, 2 * MY], BF16, kind="Internal").ap()

    es = contextlib.ExitStack()
    sbuf = lambda name, shape, dt: es.enter_context(nc.sbuf_tensor(name, shape, dt))

    XR = sbuf("xr", [P, MT, D], BF16)
    AB1 = sbuf("ab1", [P, 32768], BF16)
    AB2 = sbuf("ab2", [P, 32768], BF16)
    AF = sbuf("af", [P, D], F32)
    cmsk = sbuf("cmsk", [P, 16, P], BF16)
    ones_bf = sbuf("ones_bf", [P, P], BF16)
    wtx = sbuf("wtx", [P, 2, NT, P], BF16)
    vsl = sbuf("vsl", [P, NT, P], BF16)
    rec = sbuf("rec", [P, MY], F32)
    fstg = sbuf("fstg", [P, 512], F32)
    sqb = sbuf("sqb", [P, 512], BF16)
    rstdb = sbuf("rstdb", [P, 512], F32)
    stat = sbuf("stat", [P, 64], F32)
    gsc = sbuf("gsc", [P, 64], F32)
    gsc2 = sbuf("gsc2", [P, 64], F32)
    gscb = sbuf("gscb", [P, 64], BF16)
    tbuf = sbuf("tbuf", [P, D], BF16)
    tbuf2 = sbuf("tbuf2", [P, D], BF16)
    tbuf3 = sbuf("tbuf3", [P, 1024], BF16)
    f8b = sbuf("f8b", [P, D], F8)
    bias1 = sbuf("bias1", [P, 4, 16], F32)
    vstg = sbuf("vstg", [P, 512], BF16)
    junk = sbuf("junk", [P, 2], F32)

    psA = es.enter_context(nc.psum_tensor("psA", [P, 1024], F32))
    psB = es.enter_context(nc.psum_tensor("psB", [P, 1024], F32))
    psC = es.enter_context(nc.psum_tensor("psC", [P, 1024], F32))
    psD = es.enter_context(nc.psum_tensor("psD", [P, 1024], F32))
    PS4 = [(psA, 0), (psA, 512), (psB, 0), (psB, 512)]

    sems = {
        "dS": Ctr(es.enter_context(nc.semaphore("d_sync"))),
        "dA": Ctr(es.enter_context(nc.semaphore("d_act"))),
        "dG": Ctr(es.enter_context(nc.semaphore("d_gps"))),
        "sP": Ctr(es.enter_context(nc.semaphore("s_pe"))),
        "sV": Ctr(es.enter_context(nc.semaphore("s_dve"))),
        "sA": Ctr(es.enter_context(nc.semaphore("s_act"))),
        "sG": Ctr(es.enter_context(nc.semaphore("s_gps"))),
    }
    dS, dA, dG = sems["dS"], sems["dA"], sems["dG"]
    sP, sV, sA, sG = sems["sP"], sems["sV"], sems["sA"], sems["sG"]

    E = Emitter(sems)
    E.junk = junk

    ab1, ab2 = AB1[:], AB2[:]
    lnx_tm = ab1.rearrange("p (t d) -> p t d", t=NT)
    lnx_fm = ab2.rearrange("p (t d) -> p t d", t=NT)
    attn0 = ab1[:, 0:16384].rearrange("p (h q) -> p h q", h=16)
    kh0 = ab1[:, 16384:18432]
    vfm0 = ab1[:, 18432:20480]
    vtm0 = ab1[:, 20480:22528].rearrange("p (t q) -> p t q", t=NT)
    qh0 = ab1[:, 22528:23552]
    pt0 = ab1[:, 23552:25600].rearrange("p (b q) -> p b q", b=2)
    hfm = ab1.rearrange("p (t q) -> p t q", t=64)
    ln1fm = ab2[:, 0:8192].rearrange("p (t q) -> p t q", t=NT)
    w1c = ab2[:, 8192:24576].rearrange("p (b t q) -> p b t q", b=2, t=NT)
    w2c = ab2.rearrange("p (t q) -> p t q", t=64)
    ln2tm = ab1[:, 0:16384].rearrange("p (t d) -> p t d", t=MT)
    ln2mf = ab2[:, 0:16384].rearrange("p (t q) -> p t q", t=NT)
    q01 = ab2[:, 16384:32768].rearrange("p (h t q) -> p h t q", h=2, t=8)
    kh1 = ab2[:, 0:16384].rearrange("p (t k) -> p t k", t=8)
    fmch4 = ab1.rearrange("p (c t q) -> p c t q", c=4, t=NT)
    fmch1 = ab1[:, 0:8192].rearrange("p (t q) -> p t q", t=NT)
    pt1 = ab1[:, 0:16384].rearrange("p (t q) -> p t q", t=NT)
    attn1 = ab1[:, 16384:32768].rearrange("p (t q) -> p t q", t=NT)
    o1fm = ab2[:, 0:16384].rearrange("p (t q) -> p t q", t=NT)
    o1tm = ab2[:, 16384:32768].rearrange("p (t d) -> p t d", t=MT)
    xdtm = ab1[:, 0:16384].rearrange("p (t d) -> p t d", t=MT)
    xdfm = ab1[:, 16384:32768].rearrange("p (t q) -> p t q", t=NT)
    wdc = ab2[:, 0:16384].rearrange("p (b t q) -> p b t q", b=2, t=NT)

    x0t = x0.rearrange("(t p) d -> t p d", p=P)
    outt = out.rearrange("(t p) d -> t p d", p=P)

    tern_jobs = ([("tern_fc", i, 0) for i in range(8)]
                 + [("tern_pr", i, c) for i in range(2) for c in range(4)]
                 + [("fp8w", i, 0) for i in range(2)])

    wq0f_t = wq0f.rearrange("(t p) d -> p t d", p=P)
    wk0f_t = wk0f.rearrange("(t p) d -> p t d", p=P)
    wv0f_t = wv0f.rearrange("(t p) d -> p t d", p=P)
    wo0f_t = wo0f.rearrange("(t p) d -> p t d", p=P)
    wq1_t = wq1.rearrange("(t p) d -> p t d", p=P)
    wk1_t = wk1.rearrange("(t p) d -> p t d", p=P)
    wv1_t = wv1.rearrange("(t p) d -> p t d", p=P)
    wo1_t = wo1.rearrange("(t p) d -> p t d", p=P)
    w1tf_t = w1tf.rearrange("(r t p) q -> p r t q", p=P, t=NT)
    w2tf_t = w2tf.rearrange("(r t p) q -> p r t q", p=P, t=64)
    wd1f_t = wd1f.rearrange("(t p) d -> p t d", p=P)
    ln2f_t = ln2f.rearrange("(r t p) q -> p r t q", p=P, t=NT)
    v1fm_t = v1fm.rearrange("(t p) q -> p t q", p=P)

    sections = []

    def sec(engine):
        def deco(fn):
            sections.append((engine, fn))
            return fn
        return deco

    def emit_ln_tiles(tag, n_tiles, src_ap_fn, dst_ap_fn, out_f32=False,
                      xr_copy_from=None):
        @sec("scalar")
        def _(s):
            for i in range(n_tiles):
                E.wait(s, sV, f"{tag}_src{i}")
                s.activation(tbuf3[:, 0:D], src_ap_fn(i), ACTF.Square,
                             accum_out=stat[:, 16 + (i % 16):17 + (i % 16)])
                E.inc(s, sA)
                E.mark(f"{tag}_sq{i}", sA)
                E.wait(s, sV, f"{tag}_pre{i}")
                s.activation(stat[:, 32 + (i % 16):33 + (i % 16)],
                             stat[:, 32 + (i % 16):33 + (i % 16)], ACTF.Sqrt)
                E.inc(s, sA)
                E.mark(f"{tag}_sqrt{i}", sA)

        @sec("vector")
        def _(v):
            for i in range(n_tiles):
                c = i % 16
                E.wait(v, sA, f"{tag}_ready{i}")
                src = src_ap_fn(i)
                v.reduce_sum(stat[:, c:c + 1], src, AX.X)
                v.drain()
                v.tensor_scalar(stat[:, 48 + c:49 + c], stat[:, c:c + 1],
                                1.0 / D, 0.0, ALU.mult, ALU.add)
                E.inc(v, sV)
                E.mark(f"{tag}_src{i}", sV)
                v.drain()
                v.scalar_tensor_tensor(stat[:, 32 + c:33 + c],
                                       stat[:, 48 + c:49 + c], -1.0,
                                       stat[:, 48 + c:49 + c], ALU.mult, ALU.mult)
                v.drain()
                E.wait(v, sA, f"{tag}_sq{i}")
                v.scalar_tensor_tensor(stat[:, 32 + c:33 + c],
                                       stat[:, 16 + c:17 + c], 1.0 / D,
                                       stat[:, 32 + c:33 + c], ALU.mult, ALU.add)
                v.drain()
                v.tensor_scalar(stat[:, 32 + c:33 + c], stat[:, 32 + c:33 + c],
                                LN_EPS, 1.0, ALU.add, ALU.mult)
                v.drain()
                v.reciprocal(stat[:, 32 + c:33 + c], stat[:, 32 + c:33 + c])
                E.inc(v, sV)
                E.mark(f"{tag}_pre{i}", sV)
                E.wait(v, sA, f"{tag}_sqrt{i}")
                v.tensor_scalar(dst_ap_fn(i), src,
                                stat[:, 48 + c:49 + c], stat[:, 32 + c:33 + c],
                                ALU.subtract, ALU.mult)
                E.inc(v, sV)
                E.mark(f"{tag}_norm{i}", sV)
                if xr_copy_from is not None and i < MT:
                    v.tensor_copy(XR[:, i, :], src)
                    E.inc(v, sV)
                    E.mark(f"{tag}_xr{i}", sV)

    @sec("vector")
    def _(v):
        v.memset(ones_bf[:], 1.0)
        E.inc(v, sV)
        E.mark("ones", sV)

    @sec("sync")
    def _(s):
        s.dma_start(out=cmsk[:], in_=cmask.rearrange("j p q -> p j q")).then_inc(dS.h, 16)
        dS.n += 16
        E.mark("cmask", dS)

    @sec("gpsimd")
    def _(g):
        for i, bsrc in enumerate([bq1, bk1, bv1, bo1]):
            g.dma_start(out=bias1[:, i, :], in_=bsrc).then_inc(dG.h, 16)
            dG.n += 16
        for wsrc, wint in [(wq0m, wq0i), (wk0m, wk0i), (wv0m, wv0i), (wo0m, wo0i)]:
            src2 = wsrc.rearrange("(t p) d -> p t d", p=P)
            for j in range(2):
                g.dma_start(out=wtx[:, j, :, :],
                            in_=src2[:, j, :]
                            .rearrange("p (t q) -> p t q", t=NT)).then_inc(dG.h, 16)
                dG.n += 16
            g.wait_ge(dG.h, dG.n)
            g.dma_start(out=wint.rearrange("(t p) d -> p t d", p=P),
                        in_=wtx[:].rearrange("p a t q -> p a (t q)")).then_inc(dG.h, 16)
            dG.n += 16
            g.wait_ge(dG.h, dG.n)
        g.wait_ge(dG.h, dG.n)
        for ci, co in [(wq0i, wq0f), (wk0i, wk0f), (wv0i, wv0f), (wo0i, wo0f)]:
            cc = g.collective_compute(
                "AllGather", ALU.bypass,
                replica_groups=[list(range(8))],
                ins=[ci], outs=[co])
            E.inc(cc, sG)
        E.mark("coll1", sG)

    @sec("sync")
    def _(s):
        for t in range(NT):
            if t >= 1:
                E.wait(s, sV, f"ln0_done{t - 1}")
            s.dma_start(out=AF[:], in_=x0t[t]).then_inc(dA.h, 16)
            dA.n += 16
            E.mark(f"x0_{t}", dA)

    @sec("scalar")
    def _(s):
        for t in range(NT):
            E.wait(s, dA, f"x0_{t}")
            s.activation(tbuf3[:], AF[:, 0:1024], ACTF.Square,
                         accum_out=stat[:, 16:17])
            s.activation(tbuf3[:], AF[:, 1024:D], ACTF.Square,
                         accum_out=stat[:, 17:18])
            E.inc(s, sA)
            E.mark(f"ln0_sq{t}", sA)
            E.wait(s, sV, f"ln0_pre{t}")
            s.activation(stat[:, 32:33], stat[:, 32:33], ACTF.Sqrt)
            E.inc(s, sA)
            E.mark(f"ln0_sqrt{t}", sA)

    @sec("vector")
    def _(v):
        for t in range(NT):
            E.wait(v, dA, f"x0_{t}")
            v.reduce_sum(stat[:, 0:1], AF[:], AX.X)
            v.drain()
            v.tensor_scalar(stat[:, 48:49], stat[:, 0:1], 1.0 / D, 0.0,
                            ALU.mult, ALU.add)
            v.drain()
            v.scalar_tensor_tensor(stat[:, 32:33], stat[:, 48:49], -1.0,
                                   stat[:, 48:49], ALU.mult, ALU.mult)
            v.drain()
            E.wait(v, sA, f"ln0_sq{t}")
            v.scalar_tensor_tensor(stat[:, 32:33], stat[:, 16:17], 1.0 / D,
                                   stat[:, 32:33], ALU.mult, ALU.add)
            v.drain()
            v.scalar_tensor_tensor(stat[:, 32:33], stat[:, 17:18], 1.0 / D,
                                   stat[:, 32:33], ALU.mult, ALU.add)
            v.drain()
            v.tensor_scalar(stat[:, 32:33], stat[:, 32:33], LN_EPS, 1.0,
                            ALU.add, ALU.mult)
            v.drain()
            v.reciprocal(stat[:, 32:33], stat[:, 32:33])
            E.inc(v, sV)
            E.mark(f"ln0_pre{t}", sV)
            E.wait(v, sA, f"ln0_sqrt{t}")
            v.tensor_scalar(lnx_tm[:, t, :], AF[:], stat[:, 48:49],
                            stat[:, 32:33], ALU.subtract, ALU.mult)
            E.inc(v, sV)
            if t < MT:
                v.tensor_copy(XR[:, t, :], AF[:])
                E.inc(v, sV)
            E.mark(f"ln0_done{t}", sV)
        E.mark("ln0_all", sV)

    @sec("sync")
    def _(s):
        E.wait(s, sV, "ln0_all")
        for t in range(NT):
            s.dma_start(out=lnx_fm[:, :, t * P:(t + 1) * P],
                        in_=lnx_tm[:, t, :],
                        transpose=True).then_inc(dS.h, 16)
            dS.n += 16
        E.mark("lnxfm", dS)

    def emit_tern_store(s, j):
        kind, idx, cidx = tern_jobs[j]
        E.wait(s, sV, f"tj{j}_done")
        s.wait_ge(dA.h, dA.n)
        if kind == "tern_fc":
            s.dma_start(out=vsl[:],
                        in_=tbuf2[:].rearrange("p (t q) -> p t q", t=NT),
                        transpose=True).then_inc(dA.h, 16)
            dA.n += 16
            s.wait_ge(dA.h, dA.n)
            s.dma_start(out=w1ti.rearrange("(t p) q -> p t q", p=P)
                        [:, :, idx * P:(idx + 1) * P],
                        in_=vsl[:]).then_inc(dA.h, 16)
            dA.n += 16
        elif kind == "tern_pr":
            s.dma_start(out=vsl[:],
                        in_=tbuf2[:].rearrange("p (t q) -> p t q", t=NT),
                        transpose=True).then_inc(dA.h, 16)
            dA.n += 16
            s.wait_ge(dA.h, dA.n)
            s.dma_start(out=w2ti.rearrange("(c t p) q -> p c t q", p=P, t=NT)
                        [:, cidx, :, idx * P:(idx + 1) * P],
                        in_=vsl[:]).then_inc(dA.h, 16)
            dA.n += 16
        else:
            s.dma_start(out=wd1i.rearrange("(t p) d -> p t d", p=P)[:, idx, :],
                        in_=tbuf2[:]).then_inc(dA.h, 16)
            dA.n += 16
        E.mark(f"tj{j}_stored", dA)

    @sec("sync")
    def _(s):
        E.wait(s, sG, "coll1")
        E.wait(s, sV, "ln0_all")
        E.wait(s, dS, "lnxfm")
        for h in range(NH0 := 16):
            E.wait(s, sP, f"h{h - 1}_kproj") if h > 0 else None
            if h > 0:
                pass
            s.dma_start(out=wtx[:, 0, :, :],
                        in_=wk0f_t[:, h, :].rearrange("p (t q) -> p t q", t=NT),
                        transpose=True).then_inc(dS.h, 16)
            dS.n += 16
            E.mark(f"h{h}_wk", dS)
            E.wait(s, sP, f"h{h - 1}_vproj") if h > 0 else None
            s.dma_start(out=wtx[:, 1, :, :],
                        in_=wv0f_t[:, h, :].rearrange("p (t q) -> p t q", t=NT),
                        transpose=True).then_inc(dS.h, 16)
            dS.n += 16
            E.mark(f"h{h}_wv", dS)
            E.wait(s, sP, f"h{h}_kproj")
            s.dma_start(out=wtx[:, 0, :, :],
                        in_=wq0f_t[:, h, :].rearrange("p (t q) -> p t q", t=NT),
                        transpose=True).then_inc(dS.h, 16)
            dS.n += 16
            E.mark(f"h{h}_wq", dS)
            E.wait(s, sA, f"h{h}_vevac3")
            s.dma_start(out=vtm0[:],
                        in_=vfm0.rearrange("p (t q) -> p t q", t=NT),
                        transpose=True).then_inc(dS.h, 16)
            dS.n += 16
            E.mark(f"h{h}_vtm", dS)
            if h >= 1:
                emit_tern_store(s, h - 1)
        for j in range(15, len(tern_jobs)):
            emit_tern_store(s, j)

    @sec("tensor")
    def _(te):
        for h in range(16):
            E.wait(te, dS, f"h{h}_wk")
            if h > 0:
                E.wait(te, sA, f"h{h - 1}_exp15")
                E.wait(te, sV, f"h{h - 1}_atevac")
            for c in range(4):
                ps, off = PS4[c]
                if h > 0:
                    E.wait(te, sA, f"h{h - 1}_qevac1")
                for dt in range(NT):
                    te.matmul(ps[:, off:off + 512], wtx[:, 0, dt, :],
                              lnx_fm[:, dt, c * 512:(c + 1) * 512],
                              start=(dt == 0), stop=(dt == NT - 1))
            E.inc(te.engine_nop() if False else te.nop(), sP) if False else None
            E.inc(te, sP)
            E.mark(f"h{h}_kproj", sP)
            E.wait(te, dS, f"h{h}_wv")
            for c in range(4):
                ps, off = PS4[c]
                E.wait(te, sA, f"h{h}_kevac{c}")
                for dt in range(NT):
                    te.matmul(ps[:, off:off + 512], wtx[:, 1, dt, :],
                              lnx_fm[:, dt, c * 512:(c + 1) * 512],
                              start=(dt == 0), stop=(dt == NT - 1))
            E.inc(te, sP)
            E.mark(f"h{h}_vproj", sP)
            E.wait(te, dS, f"h{h}_wq")
            for c in range(2):
                ps, off = PS4[c]
                E.wait(te, sA, f"h{h}_vevac{c}")
                for dt in range(NT):
                    te.matmul(ps[:, off:off + 512], wtx[:, 0, dt, :],
                              lnx_fm[:, dt, c * 512:(c + 1) * 512],
                              start=(dt == 0), stop=(dt == NT - 1))
            E.inc(te, sP)
            E.mark(f"h{h}_qproj", sP)
            E.wait(te, dS, f"h{h}_vtm")
            E.wait(te, dS, "cmask")
            for kp in range(16):
                qoff, pieces = SC_PIECES[kp]
                scps = psA if (kp % 2 == 0) else psB
                if kp >= 2:
                    E.wait(te, sA, f"h{h}_exp{kp - 2}")
                else:
                    E.wait(te, sA, f"h{h}_qevac1")
                for (c0, c1) in pieces:
                    te.matmul(scps[:, c0:c1], kh0[:, kp * P:(kp + 1) * P],
                              qh0[:, c0:c1], start=True, stop=True)
                E.inc(te, sP)
                E.mark(f"h{h}_sc{kp}", sP)
                E.wait(te, sV, f"h{h}_mask{kp}")
                for (c0, c1) in pieces:
                    te.matmul(psC[:, c0:c1], vtm0[:, kp, :],
                              pt0[:, kp % 2, c0:c1],
                              start=(kp == 0), stop=(kp == 15),
                              skip_group_check=True)
                for (c0, c1) in pieces:
                    te.matmul(psD[:, c0:c1], ones_bf[:], pt0[:, kp % 2, c0:c1],
                              start=(kp == 0), stop=(kp == 15),
                              skip_group_check=True)
                E.inc(te, sP)
                E.mark(f"h{h}_av{kp}", sP)

    @sec("scalar")
    def _(s):
        for h in range(16):
            for c in range(4):
                E.wait(s, sP, f"h{h}_kproj")
                ps, off = PS4[c]
                s.activation(kh0[:, c * 512:(c + 1) * 512], ps[:, off:off + 512],
                             ACTF.Copy)
                E.inc(s, sA)
                E.mark(f"h{h}_kevac{c}", sA)
            for c in range(4):
                E.wait(s, sP, f"h{h}_vproj")
                ps, off = PS4[c]
                s.activation(vfm0[:, c * 512:(c + 1) * 512], ps[:, off:off + 512],
                             ACTF.Copy)
                E.inc(s, sA)
                E.mark(f"h{h}_vevac{c}", sA)
            for c in range(2):
                E.wait(s, sP, f"h{h}_qproj")
                ps, off = PS4[c]
                s.activation(qh0[:, c * 512:(c + 1) * 512], ps[:, off:off + 512],
                             ACTF.Copy)
                E.inc(s, sA)
                E.mark(f"h{h}_qevac{c}", sA)
            for kp in range(16):
                qoff, pieces = SC_PIECES[kp]
                scps = psA if (kp % 2 == 0) else psB
                E.wait(s, sP, f"h{h}_sc{kp}")
                if kp >= 2:
                    E.wait(s, sP, f"h{h}_av{kp - 2}")
                s.activation(pt0[:, kp % 2, qoff:1024], scps[:, qoff:1024],
                             ACTF.Exp, scale=INV_SQRT_DK0)
                E.inc(s, sA)
                E.mark(f"h{h}_exp{kp}", sA)

    def emit_tern_dve(v, j):
        kind, idx, cidx = tern_jobs[j]
        E.wait(v, dG, f"tj{j}_load")
        if j >= 1:
            E.wait(v, dA, f"tj{j - 1}_stored")
        if kind in ("tern_fc", "tern_pr"):
            wb3 = tbuf[:].rearrange("p (g e) -> p g e", e=64)
            G = D // 64
            v.reduce_sum(gsc[:, 0:G], wb3, AX.X, apply_absolute_value=True)
            v.drain()
            v.tensor_scalar(gscb[:, 0:G], gsc[:, 0:G], 1.0 / 64, 0.0,
                            ALU.mult, ALU.add)
            v.drain()
            v.tensor_scalar(gscb[:, 0:G], gscb[:, 0:G], BF16_1EM8, 0.0,
                            ALU.max, ALU.add)
            v.drain()
            v.tensor_scalar(gsc2[:, 0:G], gscb[:, 0:G], TAU_F, 0.0,
                            ALU.mult, ALU.add)
            v.drain()
            v.tensor_scalar(gsc[:, 0:G], gscb[:, 0:G], -TAU_F, 0.0,
                            ALU.mult, ALU.add)
            v.drain()
            taub = gsc2[:, 0:G].unsqueeze(2).broadcast_to([P, G, 64])
            ntaub = gsc[:, 0:G].unsqueeze(2).broadcast_to([P, G, 64])
            scb = gscb[:, 0:G].unsqueeze(2).broadcast_to([P, G, 64])
            p3 = tbuf2[:].rearrange("p (g e) -> p g e", e=64)
            v.tensor_tensor(p3, wb3, taub, ALU.is_gt)
            v.drain()
            v.tensor_tensor(p3, p3, scb, ALU.mult)
            v.drain()
            v.tensor_tensor(wb3, wb3, ntaub, ALU.is_lt)
            v.drain()
            v.tensor_tensor(wb3, wb3, scb, ALU.mult)
            v.drain()
            v.tensor_tensor(p3, p3, wb3, ALU.subtract)
            E.inc(v, sV)
        else:
            G = D // P
            af3 = AF[:].rearrange("p (g e) -> p g e", e=P)
            v.reduce_max(gsc[:, 0:G], af3, AX.X, apply_absolute_value=True)
            v.drain()
            v.tensor_scalar(gsc2[:, 0:G], gsc[:, 0:G], 1.0 / 448.0, 0.0,
                            ALU.mult, ALU.add)
            v.drain()
            v.reciprocal(gsc[:, 0:G], gsc2[:, 0:G])
            v.drain()
            v.tensor_scalar(gsc[:, 0:G], gsc[:, 0:G], 0.5, 0.0,
                            ALU.mult, ALU.add)
            v.drain()
            r2b = gsc[:, 0:G].unsqueeze(2).broadcast_to([P, G, P])
            sb_ = gsc2[:, 0:G].unsqueeze(2).broadcast_to([P, G, P])
            f83 = f8b[:].rearrange("p (g e) -> p g e", e=P)
            q3 = tbuf2[:].rearrange("p (g e) -> p g e", e=P)
            v.tensor_tensor(f83, af3, r2b, ALU.mult)
            v.drain()
            v.scalar_tensor_tensor(q3, f83, 2.0, sb_, ALU.mult, ALU.mult)
            E.inc(v, sV)
        E.mark(f"tj{j}_done", sV)

    @sec("vector")
    def _(v):
        for h in range(16):
            for kp in range(16):
                qoff, _ = SC_PIECES[kp]
                E.wait(v, sA, f"h{h}_exp{kp}")
                v.tensor_tensor(pt0[:, kp % 2, qoff:qoff + P],
                                pt0[:, kp % 2, qoff:qoff + P],
                                cmsk[:, kp, :], ALU.mult)
                E.inc(v, sV)
                E.mark(f"h{h}_mask{kp}", sV)
            E.wait(v, sP, f"h{h}_av15")
            v.reciprocal(rec[:], psD[:])
            v.drain()
            v.tensor_tensor(attn0[:, h, :], psC[:], rec[:], ALU.mult)
            E.inc(v, sV)
            E.mark(f"h{h}_atevac", sV)
            if h < len(tern_jobs):
                emit_tern_dve(v, h)
        for j in range(16, len(tern_jobs)):
            emit_tern_dve(v, j)

    @sec("gpsimd")
    def _(g):
        E.wait(g, sG, "coll1")
        for j, (kind, idx, cidx) in enumerate(tern_jobs):
            if j >= 1:
                E.wait(g, sV, f"tj{j - 1}_done")
            if kind == "tern_fc":
                src = wfcm.rearrange("(t p) d -> p t d", p=P)[:, idx, :]
                g.dma_start(out=tbuf[:], in_=src).then_inc(dG.h, 16)
            elif kind == "tern_pr":
                src = wprm.rearrange("(t p) d -> p t d", p=P)[:, idx,
                                                              cidx * D:(cidx + 1) * D]
                g.dma_start(out=tbuf[:], in_=src).then_inc(dG.h, 16)
            else:
                src = wf1m.rearrange("(t p) d -> p t d", p=P)[:, idx, :]
                E.wait(g, sV, "ln0_all")
                g.dma_start(out=AF[:], in_=src).then_inc(dG.h, 16)
            dG.n += 16
            E.mark(f"tj{j}_load", dG)
        E.wait(g, dA, f"tj{len(tern_jobs) - 1}_stored")
        for ci, co in [(w1ti, w1tf), (w2ti, w2tf), (wd1i, wd1f)]:
            cc = g.collective_compute(
                "AllGather", ALU.bypass,
                replica_groups=[list(range(8))],
                ins=[ci], outs=[co])
            E.inc(cc, sG)
        E.mark("coll2", sG)

    @sec("sync")
    def _(s):
        for dc in range(4):
            E.wait(s, sV, "h15_atevac")
            if dc > 0:
                E.wait(s, sP, f"oproj_mm{dc - 1}")
            for i in range(4):
                s.dma_start(out=wdc[:, dc % 2, :, i * P:(i + 1) * P],
                            in_=wo0f_t[:, dc * 4 + i, :]
                            .rearrange("p (t q) -> p t q", t=NT),
                            transpose=True).then_inc(dS.h, 16)
                dS.n += 16
            E.mark(f"oproj_w{dc}", dS)

    @sec("tensor")
    def _(te):
        E.wait(te, sA, "h15_exp15")
        E.wait(te, sV, "h15_atevac")
        for dc in range(4):
            E.wait(te, dS, f"oproj_w{dc}")
            for qt in range(MT):
                ps, off = PS4[qt % 4]
                if dc > 0 or qt >= 4:
                    pj = (dc * 8 + qt) - 4
                    E.wait(te, sV, f"oproj_ev{pj}")
                for dt in range(NT):
                    te.matmul(ps[:, off:off + 512], attn0[:, dt, qt * P:(qt + 1) * P],
                              wdc[:, dc % 2, dt, :],
                              start=(dt == 0), stop=(dt == NT - 1))
                E.inc(te, sP)
                E.mark(f"oproj_mm{dc}_{qt}", sP)
            E.mark(f"oproj_mm{dc}", sP)

    @sec("vector")
    def _(v):
        for dc in range(4):
            for qt in range(MT):
                ps, off = PS4[qt % 4]
                E.wait(v, sP, f"oproj_mm{dc}_{qt}")
                v.tensor_tensor(XR[:, qt, dc * 512:(dc + 1) * 512],
                                ps[:, off:off + 512],
                                XR[:, qt, dc * 512:(dc + 1) * 512], ALU.add)
                E.inc(v, sV)
                E.mark(f"oproj_ev{dc * 8 + qt}", sV)
        E.mark("oproj_done", sV)

    for half in range(2):
        @sec("scalar")
        def _(s, half=half):
            for i in range(4):
                t = half * 4 + i
                E.wait(s, sV, f"ln1_{half}_src{i}")
                s.activation(tbuf3[:], XR[:, t, 0:1024], ACTF.Square,
                             accum_out=stat[:, 16:17])
                s.activation(tbuf3[:], XR[:, t, 1024:D], ACTF.Square,
                             accum_out=stat[:, 17:18])
                E.inc(s, sA)
                E.mark(f"ln1_{half}_sq{i}", sA)
                E.wait(s, sV, f"ln1_{half}_pre{i}")
                s.activation(stat[:, 32:33], stat[:, 32:33], ACTF.Sqrt)
                E.inc(s, sA)
                E.mark(f"ln1_{half}_sqrt{i}", sA)
                E.wait(s, sV, f"ln1_{half}_norm{i}")
                if half == 0 and i == 0:
                    E.wait(s, sG, "coll2")
                s.dma_start(out=ln1fm[:, :, i * P:(i + 1) * P],
                            in_=tbuf2[:].rearrange("p (t q) -> p t q", t=NT),
                            transpose=True).then_inc(dA.h, 16)
                dA.n += 16
                E.mark(f"ln1T_{half}_{i}", dA)
                E.mark(f"ln1T_{half}_m1_{i + 1}", dA)
            E.mark(f"ln1T_{half}", dA)

        @sec("vector")
        def _(v, half=half):
            for i in range(4):
                t = half * 4 + i
                if half == 0 and i == 0:
                    E.wait(v, sV, "oproj_done")
                if half == 1:
                    E.wait(v, sV, f"ffn_{0}_done")
                E.wait(v, dA, f"ln1T_{half}_{i - 1}") if i > 0 else None
                if i > 0:
                    pass
                v.reduce_sum(stat[:, 0:1], XR[:, t, :], AX.X)
                E.inc(v, sV)
                E.mark(f"ln1_{half}_src{i}", sV)
                v.drain()
                v.tensor_scalar(stat[:, 48:49], stat[:, 0:1], 1.0 / D, 0.0,
                                ALU.mult, ALU.add)
                v.drain()
                v.scalar_tensor_tensor(stat[:, 32:33], stat[:, 48:49], -1.0,
                                       stat[:, 48:49], ALU.mult, ALU.mult)
                v.drain()
                E.wait(v, sA, f"ln1_{half}_sq{i}")
                v.scalar_tensor_tensor(stat[:, 32:33], stat[:, 16:17], 1.0 / D,
                                       stat[:, 32:33], ALU.mult, ALU.add)
                v.drain()
                v.scalar_tensor_tensor(stat[:, 32:33], stat[:, 17:18], 1.0 / D,
                                       stat[:, 32:33], ALU.mult, ALU.add)
                v.drain()
                v.tensor_scalar(stat[:, 32:33], stat[:, 32:33], LN_EPS, 1.0,
                                ALU.add, ALU.mult)
                v.drain()
                v.reciprocal(stat[:, 32:33], stat[:, 32:33])
                E.inc(v, sV)
                E.mark(f"ln1_{half}_pre{i}", sV)
                E.wait(v, sA, f"ln1_{half}_sqrt{i}")
                if i > 0:
                    E.wait(v, dA, f"ln1T_{half}_m1_{i}")
                v.tensor_scalar(tbuf2[:], XR[:, t, :], stat[:, 48:49],
                                stat[:, 32:33], ALU.subtract, ALU.mult)
                E.inc(v, sV)
                E.mark(f"ln1_{half}_norm{i}", sV)

        @sec("sync")
        def _(s, half=half):
            for hc in range(16):
                E.wait(s, sG, "coll2")
                if half == 0 and hc == 0:
                    E.wait(s, sP, "oproj_mm3")
                if half == 1 and hc < 2:
                    E.wait(s, sV, "ffn_0_done")
                if hc >= 2:
                    E.wait(s, sP, f"mm1_{half}_{hc - 2}")
                r, co = hc // 2, (hc % 2) * 512
                s.dma_start(out=w1c[:, hc % 2, :, :],
                            in_=w1tf_t[:, r, :, co:co + 512]).then_inc(dA.h, 16)
                dA.n += 16
                E.mark(f"w1c_{half}_{hc}", dA)

        @sec("tensor")
        def _(te, half=half):
            def msmm(ht2):
                E.wait(te, sV, f"sq_{half}_{ht2}")
                te.matmul(psD[:, 0:512], ones_bf[:], sqb[:],
                          start=(ht2 == 0), stop=(ht2 == 63),
                          skip_group_check=True)
                E.inc(te, sP)
                E.mark(f"msmm_{half}_{ht2}", sP)

            E.wait(te, dA, f"ln1T_{half}")
            for hc in range(16):
                E.wait(te, dA, f"w1c_{half}_{hc}")
                for hsub in range(4):
                    ht = hc * 4 + hsub
                    ps, off = PS4[ht % 4]
                    if ht >= 4:
                        E.wait(te, sV, f"h2_{half}_{ht - 4}")
                    for dt in range(NT):
                        te.matmul(ps[:, off:off + 512],
                                  w1c[:, hc % 2, dt, hsub * P:(hsub + 1) * P],
                                  ln1fm[:, dt, :],
                                  start=(dt == 0), stop=(dt == NT - 1))
                    E.inc(te, sP)
                    E.mark(f"mm1_{half}_{hc}_{hsub}", sP)
                    if ht >= 4:
                        msmm(ht - 4)
                E.mark(f"mm1_{half}_{hc}", sP)
            for ht2 in range(60, 64):
                msmm(ht2)
            for dc in range(4):
                E.wait(te, dA, f"w2c_{half}_{dc}")
                for tt in range(4):
                    ps, off = PS4[(dc * 4 + tt) % 4]
                    pj = dc * 4 + tt - 4
                    if pj >= 0:
                        E.wait(te, sV, f"mm2ev_{half}_{pj}")
                    E.wait(te, sV, f"hn_{half}")
                    for ht in range(64):
                        te.matmul(ps[:, off:off + 512],
                                  hfm[:, ht, tt * P:(tt + 1) * P],
                                  w2c[:, ht, :],
                                  start=(ht == 0), stop=(ht == 63))
                    E.inc(te, sP)
                    E.mark(f"mm2_{half}_{dc}_{tt}", sP)

        @sec("scalar")
        def _(s, half=half):
            for hc in range(16):
                for hsub in range(4):
                    ht = hc * 4 + hsub
                    ps, off = PS4[ht % 4]
                    E.wait(s, sP, f"mm1_{half}_{hc}_{hsub}")
                    if ht >= 1:
                        E.wait(s, sV, f"h2_{half}_{ht - 1}")
                    s.activation(fstg[:], ps[:, off:off + 512], ACTF.Relu)
                    E.inc(s, sA)
                    E.mark(f"relu_{half}_{ht}", sA)
            E.wait(s, sV, f"msprep_{half}")
            s.activation(rstdb[:], rstdb[:], ACTF.Sqrt)
            E.inc(s, sA)
            E.mark(f"mssqrt_{half}", sA)

        @sec("vector")
        def _(v, half=half):
            for ht in range(64):
                E.wait(v, sA, f"relu_{half}_{ht}")
                v.tensor_tensor(hfm[:, ht, :], fstg[:], fstg[:], ALU.mult)
                E.inc(v, sV)
                E.mark(f"h2_{half}_{ht}", sV)
                v.drain()
                if ht >= 1:
                    E.wait(v, sP, f"msmm_{half}_{ht - 1}")
                v.tensor_tensor(sqb[:], hfm[:, ht, :], hfm[:, ht, :], ALU.mult)
                E.inc(v, sV)
                E.mark(f"sq_{half}_{ht}", sV)
            E.wait(v, sP, f"msmm_{half}_63")
            v.tensor_scalar(rstdb[:], psD[:, 0:512], 1.0 / HID, RMS_EPS,
                            ALU.mult, ALU.add)
            E.inc(v, sV)
            E.mark(f"msprep_{half}", sV)
            E.wait(v, sA, f"mssqrt_{half}")
            v.reciprocal(rstdb[:], rstdb[:])
            E.inc(v, sV)
            v.drain()
            for ht in range(64):
                v.tensor_tensor(hfm[:, ht, :], hfm[:, ht, :], rstdb[:], ALU.mult)
                v.drain()
            E.inc(v, sV)
            E.mark(f"hn_{half}", sV)
            for dc in range(4):
                for tt in range(4):
                    ps, off = PS4[(dc * 4 + tt) % 4]
                    E.wait(v, sP, f"mm2_{half}_{dc}_{tt}")
                    t = half * 4 + tt
                    v.tensor_tensor(XR[:, t, dc * 512:(dc + 1) * 512],
                                    ps[:, off:off + 512],
                                    XR[:, t, dc * 512:(dc + 1) * 512], ALU.add)
                    E.inc(v, sV)
                    E.mark(f"mm2ev_{half}_{dc * 4 + tt}", sV)
            E.mark(f"ffn_{half}_done", sV)

        @sec("scalar")
        def _(s, half=half):
            for dc in range(4):
                E.wait(s, sV, f"hn_{half}")
                if dc > 0:
                    E.wait(s, sP, f"mm2_{half}_{dc - 1}_3")
                for r2 in range(2):
                    s.dma_start(out=w2c[:].rearrange("p t (r q) -> p r t q", r=2)
                                [:, r2, :, :],
                                in_=w2tf_t[:, 2 * dc + r2, :, :]).then_inc(dA.h, 16)
                    dA.n += 16
                E.mark(f"w2c_{half}_{dc}", dA)

    @sec("scalar")
    def _(s):
        for i in range(MT):
            E.wait(s, sV, f"ln2_src{i}")
            s.activation(tbuf3[:], XR[:, i, 0:1024], ACTF.Square,
                         accum_out=stat[:, 16:17])
            s.activation(tbuf3[:], XR[:, i, 1024:D], ACTF.Square,
                         accum_out=stat[:, 17:18])
            E.inc(s, sA)
            E.mark(f"ln2_sq{i}", sA)
            E.wait(s, sV, f"ln2_pre{i}")
            s.activation(stat[:, 32:33], stat[:, 32:33], ACTF.Sqrt)
            E.inc(s, sA)
            E.mark(f"ln2_sqrt{i}", sA)

    @sec("vector")
    def _(v):
        E.wait(v, sV, "ffn_1_done")
        for i in range(MT):
            v.reduce_sum(stat[:, 0:1], XR[:, i, :], AX.X)
            E.inc(v, sV)
            E.mark(f"ln2_src{i}", sV)
            v.drain()
            v.tensor_scalar(stat[:, 48:49], stat[:, 0:1], 1.0 / D, 0.0,
                            ALU.mult, ALU.add)
            v.drain()
            v.scalar_tensor_tensor(stat[:, 32:33], stat[:, 48:49], -1.0,
                                   stat[:, 48:49], ALU.mult, ALU.mult)
            v.drain()
            E.wait(v, sA, f"ln2_sq{i}")
            v.scalar_tensor_tensor(stat[:, 32:33], stat[:, 16:17], 1.0 / D,
                                   stat[:, 32:33], ALU.mult, ALU.add)
            v.drain()
            v.scalar_tensor_tensor(stat[:, 32:33], stat[:, 17:18], 1.0 / D,
                                   stat[:, 32:33], ALU.mult, ALU.add)
            v.drain()
            v.tensor_scalar(stat[:, 32:33], stat[:, 32:33], LN_EPS, 1.0,
                            ALU.add, ALU.mult)
            v.drain()
            v.reciprocal(stat[:, 32:33], stat[:, 32:33])
            E.inc(v, sV)
            E.mark(f"ln2_pre{i}", sV)
            E.wait(v, sA, f"ln2_sqrt{i}")
            v.tensor_scalar(ln2tm[:, i, :], XR[:, i, :], stat[:, 48:49],
                            stat[:, 32:33], ALU.subtract, ALU.mult)
            E.inc(v, sV)
            E.mark(f"ln2_norm{i}", sV)
        E.mark("ln2_all", sV)

    @sec("sync")
    def _(s):
        E.wait(s, sV, "ln2_all")
        for t in range(MT):
            s.dma_start(out=ln2mf[:, :, t * P:(t + 1) * P],
                        in_=ln2tm[:, t, :],
                        transpose=True).then_inc(dS.h, 16)
            dS.n += 16
        E.mark("ln2mfT", dS)

    @sec("gpsimd")
    def _(g):
        E.wait(g, dS, "ln2mfT")
        g.dma_start(out=ln2i.rearrange("(t p) q -> p t q", p=P),
                    in_=ln2mf[:]).then_inc(dG.h, 16)
        dG.n += 16
        g.wait_ge(dG.h, dG.n)
        cc = g.collective_compute(
            "AllGather", ALU.bypass,
            replica_groups=[[0, 1], [2, 3], [4, 5], [6, 7]],
            ins=[ln2i], outs=[ln2f])
        E.inc(cc, sG)
        E.mark("coll3", sG)

    @sec("sync")
    def _(s):
        for h in range(2):
            for hdt in range(8):
                j = h * 8 + hdt
                E.wait(s, dS, "ln2mfT")
                if j >= 2:
                    E.wait(s, sP, f"q1mm_{j - 2}")
                s.dma_start(out=wtx[:, j % 2, :, :],
                            in_=wq1_t[:, j, :].rearrange("p (t q) -> p t q", t=NT),
                            transpose=True).then_inc(dS.h, 16)
                dS.n += 16
                E.mark(f"wq1T_{j}", dS)

    @sec("tensor")
    def _(te):
        for h in range(2):
            for hdt in range(8):
                j = h * 8 + hdt
                E.wait(te, dS, f"wq1T_{j}")
                for c in range(2):
                    ps, off = PS4[(2 * j + c) % 4]
                    pj = 2 * j + c - 4
                    if pj >= 0:
                        E.wait(te, sA, f"q1ev_{pj}")
                    for dt in range(NT):
                        te.matmul(ps[:, off:off + 512], wtx[:, j % 2, dt, :],
                                  ln2mf[:, dt, c * 512:(c + 1) * 512],
                                  start=(dt == 0), stop=(dt == NT - 1))
                    E.inc(te, sP)
                    E.mark(f"q1mmc_{2 * j + c}", sP)
                E.mark(f"q1mm_{j}", sP)

    @sec("scalar")
    def _(s):
        for h in range(2):
            for hdt in range(8):
                j = h * 8 + hdt
                for c in range(2):
                    ps, off = PS4[(2 * j + c) % 4]
                    E.wait(s, sP, f"q1mmc_{2 * j + c}")
                    s.activation(q01[:, h, hdt, c * 512:(c + 1) * 512],
                                 ps[:, off:off + 512], ACTF.Relu,
                                 bias=bias1[:, 0, j:j + 1])
                    E.inc(s, sA)
                    E.mark(f"q1ev_{2 * j + c}", sA)
        E.mark("q1done", sA)

    @sec("scalar")
    def _(s):
        E.wait(s, sG, "coll3")
        E.wait(s, sV, "ln2_all")
        E.wait(s, dS, "ln2mfT")
        for c in range(4):
            r, co = c // 2, (c % 2) * 512
            s.dma_start(out=fmch4[:, c, :, :],
                        in_=ln2f_t[:, r, :, co:co + 512]).then_inc(dA.h, 16)
            dA.n += 16
        E.mark("fm4", dA)

    @sec("sync")
    def _(s):
        for dvt in range(16):
            E.wait(s, sA, "q1done")
            if dvt >= 2:
                E.wait(s, sP, f"v1mm_{dvt - 2}")
            s.dma_start(out=wtx[:, dvt % 2, :, :],
                        in_=wv1_t[:, dvt, :].rearrange("p (t q) -> p t q", t=NT),
                        transpose=True).then_inc(dS.h, 16)
            dS.n += 16
            E.mark(f"wv1T_{dvt}", dS)

    @sec("tensor")
    def _(te):
        E.wait(te, dA, "fm4")
        for dvt in range(16):
            E.wait(te, dS, f"wv1T_{dvt}")
            for c in range(4):
                ps, off = PS4[(4 * dvt + c) % 4]
                pj = 4 * dvt + c - 4
                if pj >= 0:
                    E.wait(te, sA, f"v1ev_{pj}")
                for dt in range(NT):
                    te.matmul(ps[:, off:off + 512], wtx[:, dvt % 2, dt, :],
                              fmch4[:, c, dt, :],
                              start=(dt == 0), stop=(dt == NT - 1))
                E.inc(te, sP)
                E.mark(f"v1mmc_{4 * dvt + c}", sP)
            E.mark(f"v1mm_{dvt}", sP)

    @sec("scalar")
    def _(s):
        for dvt in range(16):
            for c in range(4):
                ps, off = PS4[(4 * dvt + c) % 4]
                E.wait(s, sP, f"v1mmc_{4 * dvt + c}")
                if 4 * dvt + c > 0:
                    E.wait(s, dG, f"v1st_{4 * dvt + c - 1}")
                s.activation(vstg[:], ps[:, off:off + 512], ACTF.Relu,
                             bias=bias1[:, 2, dvt:dvt + 1])
                E.inc(s, sA)
                E.mark(f"v1ev_{4 * dvt + c}", sA)

    @sec("gpsimd")
    def _(g):
        for dvt in range(16):
            for c in range(4):
                E.wait(g, sA, f"v1ev_{4 * dvt + c}")
                g.dma_start(out=v1fm_t[:, dvt, c * 512:(c + 1) * 512],
                            in_=vstg[:]).then_inc(dG.h, 16)
                dG.n += 16
                E.mark(f"v1st_{4 * dvt + c}", dG)
        E.mark("v1done", dG)

    for h in range(2):
        @sec("gpsimd")
        def _(s, h=h):
            for kc in range(4):
                if h == 0:
                    break
                E.wait(s, sV, "h0_l1done")
                if kc >= 1:
                    E.wait(s, sP, f"k1mm_{h}_{kc - 1}")
                r, co = kc // 2, (kc % 2) * 512
                s.dma_start(out=fmch1[:, :, :],
                            in_=ln2f_t[:, r, :, co:co + 512]).then_inc(dA.h, 16)
                dA.n += 16
                E.mark(f"fm1_{h}_{kc}", dA)

        @sec("sync")
        def _(s, h=h):
            for kc in range(4):
                for hdt in range(8):
                    j2 = kc * 8 + hdt
                    E.wait(s, dG, "v1done")
                    if j2 >= 2:
                        E.wait(s, sP, f"k1mmc_{h}_{j2 - 2}")
                    s.dma_start(out=wtx[:, hdt % 2, :, :],
                                in_=wk1_t[:, h * 8 + hdt, :]
                                .rearrange("p (t q) -> p t q", t=NT),
                                transpose=True).then_inc(dS.h, 16)
                    dS.n += 16
                    E.mark(f"wk1T_{h}_{j2}", dS)

        @sec("tensor")
        def _(te, h=h):
            for kc in range(4):
                if h == 1:
                    E.wait(te, dA, f"fm1_{h}_{kc}")
                for hdt in range(8):
                    E.wait(te, dS, f"wk1T_{h}_{kc * 8 + hdt}")
                    ps, off = PS4[(kc * 8 + hdt) % 4]
                    pj = kc * 8 + hdt - 4
                    if pj >= 0:
                        E.wait(te, sA, f"k1ev_{h}_{pj}")
                    src = fmch4[:, kc, :, :] if h == 0 else fmch1[:, :, :]
                    for dt in range(NT):
                        te.matmul(ps[:, off:off + 512], wtx[:, hdt % 2, dt, :],
                                  src[:, dt, :],
                                  start=(dt == 0), stop=(dt == NT - 1))
                    E.inc(te, sP)
                    E.mark(f"k1mmc_{h}_{kc * 8 + hdt}", sP)
                    if hdt == 7:
                        E.mark(f"k1mm_{h}_{kc}", sP)
            for kp in range(16):
                scps = psA if (kp % 2 == 0) else psB
                if kp >= 2:
                    E.wait(te, sA, f"exp1_{h}_{kp - 2}")
                else:
                    E.wait(te, sA, f"k1ev_{h}_31")
                for (c0, c1) in FULL_PIECES:
                    for hdt in range(8):
                        te.matmul(scps[:, c0:c1], kh1[:, hdt, kp * P:(kp + 1) * P],
                                  q01[:, h, hdt, c0:c1],
                                  start=(hdt == 0), stop=(hdt == 7))
                E.inc(te, sP)
                E.mark(f"sc1_{h}_{kp}", sP)
                E.wait(te, sA, f"exp1_{h}_{kp}") if False else None
            for kp in range(16):
                E.wait(te, sA, f"exp1_{h}_{kp}")
                for (c0, c1) in FULL_PIECES:
                    te.matmul(psD[:, c0:c1], ones_bf[:], pt1[:, kp, c0:c1],
                              start=(kp == 0), stop=(kp == 15),
                              skip_group_check=True)
                E.inc(te, sP)
                E.mark(f"cs1_{h}_{kp}", sP)
            for dvt in range(8):
                avps = psC if (dvt % 2 == 0) else psD
                E.wait(te, dS, f"vsl_{h}_{dvt}")
                if dvt == 1:
                    E.wait(te, sV, f"rec1_{h}")
                if dvt >= 2:
                    E.wait(te, sV, f"av1ev_{h}_{dvt - 2}")
                for kp in range(16):
                    for (c0, c1) in FULL_PIECES:
                        te.matmul(avps[:, c0:c1], vsl[:, kp, :],
                                  pt1[:, kp, c0:c1],
                                  start=(kp == 0), stop=(kp == 15),
                                  skip_group_check=True)
                E.inc(te, sP)
                E.mark(f"av1_{h}_{dvt}", sP)

        @sec("scalar")
        def _(s, h=h):
            for kc in range(4):
                for hdt in range(8):
                    ps, off = PS4[(kc * 8 + hdt) % 4]
                    E.wait(s, sP, f"k1mmc_{h}_{kc * 8 + hdt}")
                    s.activation(kh1[:, hdt, kc * 512:(kc + 1) * 512],
                                 ps[:, off:off + 512], ACTF.Relu,
                                 bias=bias1[:, 1, h * 8 + hdt:h * 8 + hdt + 1])
                    E.inc(s, sA)
                    E.mark(f"k1ev_{h}_{kc * 8 + hdt}", sA)
            for kp in range(16):
                scps = psA if (kp % 2 == 0) else psB
                E.wait(s, sP, f"sc1_{h}_{kp}")
                if h == 0 and kp == 0:
                    E.wait(s, sV, "h0pt_free")
                s.activation(pt1[:, kp, :], scps[:], ACTF.Exp,
                             scale=INV_SQRT_HD1)
                E.inc(s, sA)
                E.mark(f"exp1_{h}_{kp}", sA)

        @sec("sync")
        def _(s, h=h):
            for dvt in range(8):
                E.wait(s, sP, f"cs1_{h}_15")
                if dvt >= 1:
                    E.wait(s, sP, f"av1_{h}_{dvt - 1}")
                s.dma_start(out=vsl[:],
                            in_=v1fm_t[:, h * 8 + dvt, :]
                            .rearrange("p (t q) -> p t q", t=NT),
                            transpose=True).then_inc(dS.h, 16)
                dS.n += 16
                E.mark(f"vsl_{h}_{dvt}", dS)

        @sec("vector")
        def _(v, h=h):
            if h == 0:
                E.wait(v, sV, "ffn_1_done")
                E.inc(v, sV)
                E.mark("h0pt_free", sV)
            E.wait(v, sP, f"cs1_{h}_15")
            v.reciprocal(rec[:], psD[:])
            E.inc(v, sV)
            E.mark(f"rec1_{h}", sV)
            for dvt in range(8):
                avps = psC if (dvt % 2 == 0) else psD
                E.wait(v, sP, f"av1_{h}_{dvt}")
                v.tensor_tensor(attn1[:, h * 8 + dvt, :], avps[:], rec[:],
                                ALU.mult)
                E.inc(v, sV)
                E.mark(f"av1ev_{h}_{dvt}", sV)
            E.inc(v, sV)
            E.mark(f"h{h}_l1done", sV)

    @sec("sync")
    def _(s):
        for dot in range(16):
            E.wait(s, sV, "h1_l1done")
            if dot >= 2:
                E.wait(s, sP, f"o1mm_{dot - 2}")
            s.dma_start(out=wtx[:, dot % 2, :, :],
                        in_=wo1_t[:, dot, :].rearrange("p (t q) -> p t q", t=NT),
                        transpose=True).then_inc(dS.h, 16)
            dS.n += 16
            E.mark(f"wo1T_{dot}", dS)
        for dt in range(NT):
            E.wait(s, sA, "o1evall")
            s.dma_start(out=o1tm[:, :, dt * P:(dt + 1) * P],
                        in_=o1fm[:, dt, :],
                        transpose=True).then_inc(dS.h, 16)
            dS.n += 16
        E.mark("o1tmT", dS)

    @sec("tensor")
    def _(te):
        for dot in range(16):
            E.wait(te, dS, f"wo1T_{dot}")
            for c in range(2):
                ps, off = PS4[(2 * dot + c) % 4]
                pj = 2 * dot + c - 4
                if pj >= 0:
                    E.wait(te, sA, f"o1ev_{pj}")
                for dt in range(NT):
                    te.matmul(ps[:, off:off + 512], wtx[:, dot % 2, dt, :],
                              attn1[:, dt, c * 512:(c + 1) * 512],
                              start=(dt == 0), stop=(dt == NT - 1))
                E.inc(te, sP)
                E.mark(f"o1mmc_{2 * dot + c}", sP)
            E.mark(f"o1mm_{dot}", sP)

    @sec("scalar")
    def _(s):
        for dot in range(16):
            for c in range(2):
                ps, off = PS4[(2 * dot + c) % 4]
                E.wait(s, sP, f"o1mmc_{2 * dot + c}")
                s.activation(o1fm[:, dot, c * 512:(c + 1) * 512],
                             ps[:, off:off + 512], ACTF.Relu,
                             bias=bias1[:, 3, dot:dot + 1])
                E.inc(s, sA)
                E.mark(f"o1ev_{2 * dot + c}", sA)
        E.mark("o1evall", sA)

    @sec("vector")
    def _(v):
        E.wait(v, dS, "o1tmT")
        for tt in range(MT):
            v.tensor_tensor(XR[:, tt, :], o1tm[:, tt, :], XR[:, tt, :], ALU.add)
            E.inc(v, sV)
        E.inc(v, sV)
        E.mark("l1res_done", sV)

    @sec("scalar")
    def _(s):
        for i in range(MT):
            E.wait(s, sV, f"ln3_src{i}")
            s.activation(tbuf3[:], XR[:, i, 0:1024], ACTF.Square,
                         accum_out=stat[:, 16:17])
            s.activation(tbuf3[:], XR[:, i, 1024:D], ACTF.Square,
                         accum_out=stat[:, 17:18])
            E.inc(s, sA)
            E.mark(f"ln3_sq{i}", sA)
            E.wait(s, sV, f"ln3_pre{i}")
            s.activation(stat[:, 32:33], stat[:, 32:33], ACTF.Sqrt)
            E.inc(s, sA)
            E.mark(f"ln3_sqrt{i}", sA)

    @sec("vector")
    def _(v):
        E.wait(v, sV, "l1res_done")
        for i in range(MT):
            v.reduce_sum(stat[:, 0:1], XR[:, i, :], AX.X)
            E.inc(v, sV)
            E.mark(f"ln3_src{i}", sV)
            v.drain()
            v.tensor_scalar(stat[:, 48:49], stat[:, 0:1], 1.0 / D, 0.0,
                            ALU.mult, ALU.add)
            v.drain()
            v.scalar_tensor_tensor(stat[:, 32:33], stat[:, 48:49], -1.0,
                                   stat[:, 48:49], ALU.mult, ALU.mult)
            v.drain()
            E.wait(v, sA, f"ln3_sq{i}")
            v.scalar_tensor_tensor(stat[:, 32:33], stat[:, 16:17], 1.0 / D,
                                   stat[:, 32:33], ALU.mult, ALU.add)
            v.drain()
            v.scalar_tensor_tensor(stat[:, 32:33], stat[:, 17:18], 1.0 / D,
                                   stat[:, 32:33], ALU.mult, ALU.add)
            v.drain()
            v.tensor_scalar(stat[:, 32:33], stat[:, 32:33], LN_EPS, 1.0,
                            ALU.add, ALU.mult)
            v.drain()
            v.reciprocal(stat[:, 32:33], stat[:, 32:33])
            E.inc(v, sV)
            E.mark(f"ln3_pre{i}", sV)
            E.wait(v, sA, f"ln3_sqrt{i}")
            v.tensor_scalar(AF[:], XR[:, i, :], stat[:, 48:49],
                            stat[:, 32:33], ALU.subtract, ALU.mult)
            E.inc(v, sV)
            v.drain()
            G = D // P
            af3 = AF[:].rearrange("p (g e) -> p g e", e=P)
            v.reduce_max(gsc[:, 0:G], af3, AX.X, apply_absolute_value=True)
            v.drain()
            v.tensor_scalar(gsc2[:, 0:G], gsc[:, 0:G], 1.0 / 448.0, 0.0,
                            ALU.mult, ALU.add)
            v.drain()
            v.reciprocal(gsc[:, 0:G], gsc2[:, 0:G])
            v.drain()
            v.tensor_scalar(gsc[:, 0:G], gsc[:, 0:G], 0.5, 0.0,
                            ALU.mult, ALU.add)
            v.drain()
            r2b = gsc[:, 0:G].unsqueeze(2).broadcast_to([P, G, P])
            sb_ = gsc2[:, 0:G].unsqueeze(2).broadcast_to([P, G, P])
            f83 = f8b[:].rearrange("p (g e) -> p g e", e=P)
            x3 = xdtm[:, i, :].rearrange("p (g e) -> p g e", e=P)
            v.tensor_tensor(f83, af3, r2b, ALU.mult)
            v.drain()
            v.scalar_tensor_tensor(x3, f83, 2.0, sb_, ALU.mult, ALU.mult)
            E.inc(v, sV)
            E.mark(f"xq_{i}", sV)
        E.mark("xq_all", sV)

    @sec("sync")
    def _(s):
        E.wait(s, sV, "xq_all")
        for t in range(MT):
            s.dma_start(out=xdfm[:, :, t * P:(t + 1) * P],
                        in_=xdtm[:, t, :],
                        transpose=True).then_inc(dS.h, 16)
            dS.n += 16
        E.mark("xdfmT", dS)
        for dc in range(4):
            E.wait(s, sG, "coll2")
            if dc >= 2:
                E.wait(s, sP, f"ffn1mm_{dc - 2}_7")
            for i in range(4):
                s.dma_start(out=wdc[:, dc % 2, :, i * P:(i + 1) * P],
                            in_=wd1f_t[:, dc * 4 + i, :]
                            .rearrange("p (t q) -> p t q", t=NT),
                            transpose=True).then_inc(dS.h, 16)
                dS.n += 16
            E.mark(f"wdcT_{dc}", dS)

    @sec("tensor")
    def _(te):
        E.wait(te, dS, "xdfmT")
        for dc in range(4):
            E.wait(te, dS, f"wdcT_{dc}")
            for qt in range(MT):
                ps, off = PS4[qt % 4]
                pj = dc * 8 + qt - 4
                if pj >= 0:
                    E.wait(te, sV, f"ffn1ev_{pj}")
                for dt in range(NT):
                    te.matmul(ps[:, off:off + 512], xdfm[:, dt, qt * P:(qt + 1) * P],
                              wdc[:, dc % 2, dt, :],
                              start=(dt == 0), stop=(dt == NT - 1))
                E.inc(te, sP)
                E.mark(f"ffn1mm_{dc}_{qt}", sP)

    @sec("vector")
    def _(v):
        for dc in range(4):
            for qt in range(MT):
                ps, off = PS4[qt % 4]
                E.wait(v, sP, f"ffn1mm_{dc}_{qt}")
                if dc * 8 + qt > 0:
                    E.wait(v, dG, f"ost_{dc * 8 + qt - 1}")
                v.tensor_tensor(fstg[:], ps[:, off:off + 512],
                                XR[:, qt, dc * 512:(dc + 1) * 512], ALU.add)
                E.inc(v, sV)
                E.mark(f"ffn1ev_{dc * 8 + qt}", sV)

    @sec("gpsimd")
    def _(g):
        for dc in range(4):
            for qt in range(MT):
                E.wait(g, sV, f"ffn1ev_{dc * 8 + qt}")
                g.dma_start(out=outt[qt][:, dc * 512:(dc + 1) * 512],
                            in_=fstg[:]).then_inc(dG.h, 16)
                dG.n += 16
                E.mark(f"ost_{dc * 8 + qt}", dG)
        g.wait_ge(dG.h, dG.n)

    NH0 = 16
    if _SEC_LIMIT is not None:
        del sections[_SEC_LIMIT:]
    if _DUMP is not None:
        _mark, _what = _DUMP
        _srcs = {"XR": XR[:], "lnx": lnx_tm[:, 0:MT, :],
                 "ab1": AB1[:].rearrange("p (t d) -> p t d", t=NT)[:, 0:MT, :],
                 "ab1hi": AB1[:].rearrange("p (t d) -> p t d", t=NT)[:, MT:NT, :],
                 "ab2": AB2[:].rearrange("p (t d) -> p t d", t=NT)[:, 0:MT, :],
                 "w1ti": w1ti.rearrange("(t a p) q -> p t a q", p=P, a=2),
                 "w2ti": w2ti.rearrange("(t a p) q -> p t a q", p=P, a=8)}
        _dsts = {"w1ti": out.rearrange("(t p) (a q) -> p t a q", p=P, q=1024),
                 "w2ti": out.rearrange("(t p) (a q) -> p t a q", p=P, q=256)}

        @sec("gpsimd")
        def _(g):
            E.wait(g, sV, _mark)
            if _what in _dsts:
                for _t in range(8):
                    g.dma_start(out=_dsts[_what][:, _t, :, :],
                                in_=_srcs[_what][:, _t, :, :]).then_inc(dG.h, 16)
                    dG.n += 16
            else:
                g.dma_start(out=out.rearrange("(t p) d -> p t d", p=P),
                            in_=_srcs[_what]).then_inc(dG.h, 16)
                dG.n += 16
            g.wait_ge(dG.h, dG.n)
    E.mock = True
    E.reset()
    mock = _MockEng()
    for engine, fn in sections:
        mock._engine = engine
        fn(mock)
    E.mock = False
    E.reset()
    mock = _MockEng()
    for engine, fn in sections:
        mock._engine = engine
        fn(mock)
    semname = {id(c.h): k for k, c in sems.items()}
    stuck, simsems = _simulate_queues(mock.trace)
    if stuck:
        byval = {}
        for mk, mv in E.marks.items():
            byval.setdefault(mv, []).append(mk)
        msg = []
        for e, (idx, it) in stuck.items():
            if it[0] == "wait":
                cands = [m for m in byval.get(it[2], [])]
                msg.append(f"{e}[{idx}]: wait {semname.get(it[1], it[1])}"
                           f" >= {it[2]} (cur {simsems.get(it[1], 0)})"
                           f" @lines {it[3]} marks~{cands[:6]}")
            else:
                msg.append(f"{e}[{idx}]: {it}")
        raise RuntimeError("queue deadlock detected:\n" + "\n".join(msg))
    E.mock = False
    E.reset()
    with nc.Block() as block:
        for engine, fn in sections:
            getattr(block, engine)(lambda eng, fn=fn: fn(_EngProxy(eng)))

    es.close()
    return nc


def _pack_inputs(inputs):
    f32 = np.float32
    bf16 = ml_dtypes.bfloat16
    x_all = np.ascontiguousarray(np.asarray(inputs["L0_attn_in_features"], dtype=f32))
    wq0 = np.asarray(inputs["L0_attn_q_proj"], dtype=f32)
    wk0 = np.asarray(inputs["L0_attn_k_proj"], dtype=f32)
    wv0 = np.asarray(inputs["L0_attn_v_proj"], dtype=f32)
    wo0 = np.asarray(inputs["L0_attn_o_proj"], dtype=f32)
    wfc = np.asarray(inputs["L0_ffn_W_fc"], dtype=f32)
    wpr = np.asarray(inputs["L0_ffn_W_proj"], dtype=f32)
    wf1 = np.asarray(inputs["L1_ffn_weight"], dtype=f32)

    def as_bf16(a):
        a = np.asarray(a)
        if a.dtype == np.uint16:
            return a.view(bf16)
        return a.astype(bf16)

    l1w = {k: as_bf16(inputs[k]) for k in
           ["L1_attn_Wq", "L1_attn_bq", "L1_attn_Wk", "L1_attn_bk",
            "L1_attn_Wv", "L1_attn_bv", "L1_attn_Wo", "L1_attn_bo"]}

    tri = np.tril(np.ones((P, P), dtype=np.float32)).T
    tri = tri.astype(bf16)
    zeros = np.zeros((P, P), dtype=bf16)
    ones = np.ones((P, P), dtype=bf16)

    in_maps = []
    for c in range(8):
        b, par = c // 2, c % 2
        xb = x_all[b].reshape(NT, P, D)
        order = list(range(par, NT, 2)) + list(range(1 - par, NT, 2))
        x_perm = np.ascontiguousarray(xb[order].reshape(NT * P, D))
        cm = np.empty((16, P, P), dtype=bf16)
        for kp in range(16):
            if kp < 8:
                cm[kp] = tri
            else:
                cm[kp] = zeros if par == 0 else ones
        r = c
        sl = slice(r * 256, (r + 1) * 256)
        in_maps.append({
            "x0": x_perm,
            "cmask": cm,
            "wq0m": np.ascontiguousarray(wq0[sl]),
            "wk0m": np.ascontiguousarray(wk0[sl]),
            "wv0m": np.ascontiguousarray(wv0[sl]),
            "wo0m": np.ascontiguousarray(wo0[sl]),
            "wfcm": np.ascontiguousarray(wfc[r * 1024:(r + 1) * 1024]),
            "wprm": np.ascontiguousarray(wpr[sl]),
            "wq1": l1w["L1_attn_Wq"], "wk1": l1w["L1_attn_Wk"],
            "wv1": l1w["L1_attn_Wv"], "wo1": l1w["L1_attn_Wo"],
            "bq1": np.ascontiguousarray(l1w["L1_attn_bq"].reshape(16, P).T),
            "bk1": np.ascontiguousarray(l1w["L1_attn_bk"].reshape(16, P).T),
            "bv1": np.ascontiguousarray(l1w["L1_attn_bv"].reshape(16, P).T),
            "bo1": np.ascontiguousarray(l1w["L1_attn_bo"].reshape(16, P).T),
            "wf1m": np.ascontiguousarray(wf1[sl]),
        })
    return in_maps


def kernel(**inputs):
    in_maps = _pack_inputs(inputs)
    nc = build_nc()
    res = run_bass_kernel_spmd(nc, in_maps, core_ids=list(range(8)))
    outp = np.empty((4, NT * P, D), dtype=np.float32)
    for c in range(8):
        b, par = c // 2, c % 2
        oc = np.asarray(res.results[c]["out"], dtype=np.float32).reshape(MT, P, D)
        for t in range(MT):
            g = 2 * t + par
            outp[b, g * P:(g + 1) * P, :] = oc[t]
    return outp


if __name__ == "__main__":
    nc = build_nc()
    print("build OK; instructions:",
          sum(1 for _ in nc.all_instructions()) if hasattr(nc, "all_instructions") else "?")


# version 22 (final)
# speedup vs baseline: 1.1452x; 1.1452x over previous
import math
import sys
import contextlib
import numpy as np
import ml_dtypes

import concourse.bass as bass
import concourse.mybir as mybir
from concourse.bass_utils import run_bass_kernel_spmd

F32 = mybir.dt.float32
BF16 = mybir.dt.bfloat16
F8 = mybir.dt.float8e4
ALU = mybir.AluOpType
ACTF = mybir.ActivationFunctionType
AX = mybir.AxisListType

P = 128
D = 2048
HID = 8192
NT = 16
MT = 8
MY = MT * P
INV_SQRT_DK0 = 1.0 / math.sqrt(128.0)
INV_SQRT_HD1 = 1.0 / 32.0
TAU_F = 0.5 * (1.0 + 2.0 ** -9)
BF16_1EM8 = float(ml_dtypes.bfloat16(1e-8))
RMS_EPS = float(np.finfo(np.float32).eps)
LN_EPS = 1e-5
_SEC_LIMIT = None
_TERN_LIMIT = None
_DUMP = None

SC_PIECES = []
for _kp in range(16):
    _nu = (8 - _kp) if _kp < 8 else (16 - _kp)
    _qoff = 1024 - _nu * P
    _pieces = []
    _c = _qoff
    while _c < 1024:
        _c2 = min(1024, (_c // 512 + 1) * 512)
        _pieces.append((_c, _c2))
        _c = _c2
    SC_PIECES.append((_qoff, _pieces))
FULL_PIECES = [(0, 512), (512, 1024)]


class Ctr:
    def __init__(self, handle):
        self.h = handle
        self.n = 0


class _MockInstr:
    def __init__(self, eng=None):
        self._eng = eng

    def then_inc(self, h, k):
        if self._eng is not None:
            self._eng._attach_inc(h, k)
        return self


class _MockEng:

    def __init__(self):
        self.trace = {}
        self._engine = None

    def _items(self):
        return self.trace.setdefault(self._engine, [])

    def _attach_inc(self, h, k):
        items = self._items()
        if items and items[-1][0] == "instr":
            items[-1][2].append((id(h), k))
        else:
            items.append(["instr", "nop", [(id(h), k)], 0])

    def __getattr__(self, name):
        eng = self
        if name == "wait_ge":
            def w(h, v):
                try:
                    lines = (sys._getframe(1).f_lineno,
                             sys._getframe(2).f_lineno)
                except ValueError:
                    lines = (sys._getframe(1).f_lineno,)
                eng._items().append(["wait", id(h), v, lines])
                return _MockInstr(eng)
            return w
        if name == "sem_inc":
            def si(h, k):
                eng._items().append(["instr", "sem_inc", [(id(h), k)],
                                     sys._getframe(1).f_lineno])
                return _MockInstr(eng)
            return si
        if name == "then_inc":
            def ti(h, k):
                eng._attach_inc(h, k)
                return _MockInstr(eng)
            return ti

        def call(*a, **kw):
            eng._items().append(["instr", name, [],
                                 sys._getframe(1).f_lineno])
            return _MockInstr(eng)
        return call


class _EngProxy:

    def __init__(self, real):
        object.__setattr__(self, "_real", real)
        object.__setattr__(self, "_last", None)

    def __getattr__(self, name):
        attr = getattr(self._real, name)
        if callable(attr):
            def call(*a, **kw):
                r = attr(*a, **kw)
                if name not in ("wait_ge", "drain") and hasattr(r, "then_inc"):
                    object.__setattr__(self, "_last", r)
                return r
            return call
        return attr


def _simulate_queues(trace):
    sems = {}
    heads = {e: 0 for e in trace}
    progress = True
    while progress:
        progress = False
        for e, items in trace.items():
            i = heads[e]
            while i < len(items):
                it = items[i]
                if it[0] == "wait" and sems.get(it[1], 0) < it[2]:
                    break
                if it[0] == "instr":
                    for h, k in it[2]:
                        sems[h] = sems.get(h, 0) + k
                i += 1
                progress = True
            heads[e] = i
    stuck = {e: (heads[e], trace[e][heads[e]])
             for e in trace if heads[e] < len(trace[e])}
    return stuck, sems


class Emitter:

    def __init__(self, sems):
        self.sems = sems
        self.marks = {}
        self.mark_ctr = {}
        self.mock = True

    def reset(self):
        for c in self.sems.values():
            c.n = 0

    junk = None

    def inc(self, obj, ctr, k=1):
        if isinstance(obj, _EngProxy):
            tgt = obj._last
            if tgt is None or getattr(obj, "_inced", None) is tgt:
                nm = type(obj._real).__name__.lower()
                if "vector" in nm or "dve" in nm:
                    obj.tensor_copy(self.junk[:, 0:1], self.junk[:, 1:2])
                elif "scalar" in nm or "act" in nm:
                    obj.activation(self.junk[:, 0:1], self.junk[:, 1:2],
                                   ACTF.Copy)
                else:
                    raise RuntimeError(f"double inc on engine {nm}")
                tgt = obj._last
            tgt.then_inc(ctr.h, k)
            object.__setattr__(obj, "_inced", tgt)
        elif hasattr(obj, "then_inc"):
            obj.then_inc(ctr.h, k)
        else:
            raise TypeError(f"E.inc target {obj!r} has no then_inc")
        ctr.n += k

    def mark(self, key, ctr):
        if self.mock:
            self.marks[key] = ctr.n
            self.mark_ctr[key] = id(ctr)
        else:
            assert self.marks[key] == ctr.n, (key, self.marks[key], ctr.n)

    def wait(self, eng, ctr, key):
        if key not in self.marks and not self.mock:
            if _SEC_LIMIT is not None or _TERN_LIMIT is not None:
                return
            raise AssertionError(f"wait on unrecorded mark {key}")
        if key in self.mark_ctr and self.mark_ctr[key] != id(ctr):
            raise AssertionError(f"wait on mark {key} with wrong semaphore")
        v = self.marks.get(key, 0)
        if v > 0:
            eng.wait_ge(ctr.h, v)


def build_nc():
    nc = bass.Bass("TRN2", target_bir_lowering=False)

    x0 = nc.dram_tensor("x0", [NT * P, D], F32, kind="ExternalInput").ap()
    cmask = nc.dram_tensor("cmask", [16, P, P], BF16, kind="ExternalInput").ap()
    wq0m = nc.dram_tensor("wq0m", [256, D], F32, kind="ExternalInput").ap()
    wk0m = nc.dram_tensor("wk0m", [256, D], F32, kind="ExternalInput").ap()
    wv0m = nc.dram_tensor("wv0m", [256, D], F32, kind="ExternalInput").ap()
    wo0m = nc.dram_tensor("wo0m", [256, D], F32, kind="ExternalInput").ap()
    wfcm = nc.dram_tensor("wfcm", [HID // 8, D], F32, kind="ExternalInput").ap()
    wprm = nc.dram_tensor("wprm", [D // 8, HID], F32, kind="ExternalInput").ap()
    wq1 = nc.dram_tensor("wq1", [D, D], BF16, kind="ExternalInput").ap()
    wk1 = nc.dram_tensor("wk1", [D, D], BF16, kind="ExternalInput").ap()
    wv1 = nc.dram_tensor("wv1", [D, D], BF16, kind="ExternalInput").ap()
    wo1 = nc.dram_tensor("wo1", [D, D], BF16, kind="ExternalInput").ap()
    bq1 = nc.dram_tensor("bq1", [P, 16], BF16, kind="ExternalInput").ap()
    bk1 = nc.dram_tensor("bk1", [P, 16], BF16, kind="ExternalInput").ap()
    bv1 = nc.dram_tensor("bv1", [P, 16], BF16, kind="ExternalInput").ap()
    bo1 = nc.dram_tensor("bo1", [P, 16], BF16, kind="ExternalInput").ap()
    wf1m = nc.dram_tensor("wf1m", [256, D], F32, kind="ExternalInput").ap()
    out = nc.dram_tensor("out", [MY, D], F32, kind="ExternalOutput").ap()

    wq0i = nc.dram_tensor("wq0i", [256, D], BF16, kind="Internal").ap()
    wk0i = nc.dram_tensor("wk0i", [256, D], BF16, kind="Internal").ap()
    wv0i = nc.dram_tensor("wv0i", [256, D], BF16, kind="Internal").ap()
    wo0i = nc.dram_tensor("wo0i", [256, D], BF16, kind="Internal").ap()
    wq0f = nc.dram_tensor("wq0f", [D, D], BF16, kind="Internal").ap()
    wk0f = nc.dram_tensor("wk0f", [D, D], BF16, kind="Internal").ap()
    wv0f = nc.dram_tensor("wv0f", [D, D], BF16, kind="Internal").ap()
    wo0f = nc.dram_tensor("wo0f", [D, D], BF16, kind="Internal").ap()
    w1ti = nc.dram_tensor("w1ti", [D, HID // 8], BF16, kind="Internal").ap()
    w2ti = nc.dram_tensor("w2ti", [HID, D // 8], BF16, kind="Internal").ap()
    wd1i = nc.dram_tensor("wd1i", [256, D], BF16, kind="Internal").ap()
    w1tf = nc.dram_tensor("w1tf", [8 * D, HID // 8], BF16, kind="Internal").ap()
    w2tf = nc.dram_tensor("w2tf", [8 * HID, D // 8], BF16, kind="Internal").ap()
    wd1f = nc.dram_tensor("wd1f", [D, D], BF16, kind="Internal").ap()
    ln2i = nc.dram_tensor("ln2i", [D, MY], BF16, kind="Internal").ap()
    ln2f = nc.dram_tensor("ln2f", [2 * D, MY], BF16, kind="Internal").ap()
    v1fm = nc.dram_tensor("v1fm", [D, 2 * MY], BF16, kind="Internal").ap()

    es = contextlib.ExitStack()
    sbuf = lambda name, shape, dt: es.enter_context(nc.sbuf_tensor(name, shape, dt))

    XR = sbuf("xr", [P, MT, D], BF16)
    AB1 = sbuf("ab1", [P, 32768], BF16)
    AB2 = sbuf("ab2", [P, 32768], BF16)
    AF = sbuf("af", [P, D], F32)
    cmsk = sbuf("cmsk", [P, 16, P], BF16)
    ones_bf = sbuf("ones_bf", [P, P], BF16)
    wtx = sbuf("wtx", [P, 2, NT, P], BF16)
    vsl = sbuf("vsl", [P, NT, P], BF16)
    rec = sbuf("rec", [P, MY], F32)
    fstg = sbuf("fstg", [P, 512], F32)
    sqb = sbuf("sqb", [P, 512], BF16)
    rstdb = sbuf("rstdb", [P, 512], F32)
    stat = sbuf("stat", [P, 64], F32)
    gsc = sbuf("gsc", [P, 64], F32)
    gsc2 = sbuf("gsc2", [P, 64], F32)
    gscb = sbuf("gscb", [P, 64], BF16)
    tbuf = sbuf("tbuf", [P, D], BF16)
    tbuf2 = sbuf("tbuf2", [P, D], BF16)
    tbuf3 = sbuf("tbuf3", [P, 1024], BF16)
    f8b = sbuf("f8b", [P, D], F8)
    bias1 = sbuf("bias1", [P, 4, 16], F32)
    vstg = sbuf("vstg", [P, 512], BF16)
    junk = sbuf("junk", [P, 2], F32)

    psA = es.enter_context(nc.psum_tensor("psA", [P, 1024], F32))
    psB = es.enter_context(nc.psum_tensor("psB", [P, 1024], F32))
    psC = es.enter_context(nc.psum_tensor("psC", [P, 1024], F32))
    psD = es.enter_context(nc.psum_tensor("psD", [P, 1024], F32))
    PS4 = [(psA, 0), (psA, 512), (psB, 0), (psB, 512)]

    sems = {
        "dS": Ctr(es.enter_context(nc.semaphore("d_sync"))),
        "dA": Ctr(es.enter_context(nc.semaphore("d_act"))),
        "dG": Ctr(es.enter_context(nc.semaphore("d_gps"))),
        "sP": Ctr(es.enter_context(nc.semaphore("s_pe"))),
        "sV": Ctr(es.enter_context(nc.semaphore("s_dve"))),
        "sA": Ctr(es.enter_context(nc.semaphore("s_act"))),
        "sG": Ctr(es.enter_context(nc.semaphore("s_gps"))),
    }
    dS, dA, dG = sems["dS"], sems["dA"], sems["dG"]
    sP, sV, sA, sG = sems["sP"], sems["sV"], sems["sA"], sems["sG"]

    E = Emitter(sems)
    E.junk = junk

    ab1, ab2 = AB1[:], AB2[:]
    lnx_tm = ab1.rearrange("p (t d) -> p t d", t=NT)
    lnx_fm = ab2.rearrange("p (t d) -> p t d", t=NT)
    attn0 = ab1[:, 0:16384].rearrange("p (h q) -> p h q", h=16)
    kh0 = ab1[:, 16384:18432]
    vfm0 = ab1[:, 18432:20480]
    vtm0 = ab1[:, 20480:22528].rearrange("p (t q) -> p t q", t=NT)
    qh0 = ab1[:, 22528:23552]
    pt0 = ab1[:, 23552:25600].rearrange("p (b q) -> p b q", b=2)
    hfm = ab1.rearrange("p (t q) -> p t q", t=64)
    ln1fm = ab2[:, 0:8192].rearrange("p (t q) -> p t q", t=NT)
    w1c = ab2[:, 8192:24576].rearrange("p (b t q) -> p b t q", b=2, t=NT)
    w2c = ab2.rearrange("p (t q) -> p t q", t=64)
    ln2tm = ab1[:, 0:16384].rearrange("p (t d) -> p t d", t=MT)
    ln2mf = ab2[:, 0:16384].rearrange("p (t q) -> p t q", t=NT)
    q01 = ab2[:, 16384:32768].rearrange("p (h t q) -> p h t q", h=2, t=8)
    kh1 = ab2[:, 0:16384].rearrange("p (t k) -> p t k", t=8)
    fmch4 = ab1.rearrange("p (c t q) -> p c t q", c=4, t=NT)
    fmch1 = ab1[:, 0:8192].rearrange("p (t q) -> p t q", t=NT)
    pt1 = ab1[:, 0:16384].rearrange("p (t q) -> p t q", t=NT)
    attn1 = ab1[:, 16384:32768].rearrange("p (t q) -> p t q", t=NT)
    o1fm = ab2[:, 0:16384].rearrange("p (t q) -> p t q", t=NT)
    o1tm = ab2[:, 16384:32768].rearrange("p (t d) -> p t d", t=MT)
    xdtm = ab1[:, 0:16384].rearrange("p (t d) -> p t d", t=MT)
    xdfm = ab1[:, 16384:32768].rearrange("p (t q) -> p t q", t=NT)
    wdc = ab2[:, 0:16384].rearrange("p (b t q) -> p b t q", b=2, t=NT)

    x0t = x0.rearrange("(t p) d -> t p d", p=P)
    outt = out.rearrange("(t p) d -> t p d", p=P)

    tern_jobs = ([("tern_fc", i, 0) for i in range(8)]
                 + [("tern_pr", i, c) for i in range(2) for c in range(4)]
                 + [("fp8w", i, 0) for i in range(2)])

    wq0f_t = wq0f.rearrange("(t p) d -> p t d", p=P)
    wk0f_t = wk0f.rearrange("(t p) d -> p t d", p=P)
    wv0f_t = wv0f.rearrange("(t p) d -> p t d", p=P)
    wo0f_t = wo0f.rearrange("(t p) d -> p t d", p=P)
    wq1_t = wq1.rearrange("(t p) d -> p t d", p=P)
    wk1_t = wk1.rearrange("(t p) d -> p t d", p=P)
    wv1_t = wv1.rearrange("(t p) d -> p t d", p=P)
    wo1_t = wo1.rearrange("(t p) d -> p t d", p=P)
    w1tf_t = w1tf.rearrange("(r t p) q -> p r t q", p=P, t=NT)
    w2tf_t = w2tf.rearrange("(r t p) q -> p r t q", p=P, t=64)
    wd1f_t = wd1f.rearrange("(t p) d -> p t d", p=P)
    ln2f_t = ln2f.rearrange("(r t p) q -> p r t q", p=P, t=NT)
    v1fm_t = v1fm.rearrange("(t p) q -> p t q", p=P)

    sections = []

    def sec(engine):
        def deco(fn):
            sections.append((engine, fn))
            return fn
        return deco

    def emit_ln_tiles(tag, n_tiles, src_ap_fn, dst_ap_fn, out_f32=False,
                      xr_copy_from=None):
        @sec("scalar")
        def _(s):
            for i in range(n_tiles):
                E.wait(s, sV, f"{tag}_src{i}")
                s.activation(tbuf3[:, 0:D], src_ap_fn(i), ACTF.Square,
                             accum_out=stat[:, 16 + (i % 16):17 + (i % 16)])
                E.inc(s, sA)
                E.mark(f"{tag}_sq{i}", sA)
                E.wait(s, sV, f"{tag}_pre{i}")
                s.activation(stat[:, 32 + (i % 16):33 + (i % 16)],
                             stat[:, 32 + (i % 16):33 + (i % 16)], ACTF.Sqrt)
                E.inc(s, sA)
                E.mark(f"{tag}_sqrt{i}", sA)

        @sec("vector")
        def _(v):
            for i in range(n_tiles):
                c = i % 16
                E.wait(v, sA, f"{tag}_ready{i}")
                src = src_ap_fn(i)
                v.reduce_sum(stat[:, c:c + 1], src, AX.X)
                v.drain()
                v.tensor_scalar(stat[:, 48 + c:49 + c], stat[:, c:c + 1],
                                1.0 / D, 0.0, ALU.mult, ALU.add)
                E.inc(v, sV)
                E.mark(f"{tag}_src{i}", sV)
                v.drain()
                v.scalar_tensor_tensor(stat[:, 32 + c:33 + c],
                                       stat[:, 48 + c:49 + c], -1.0,
                                       stat[:, 48 + c:49 + c], ALU.mult, ALU.mult)
                v.drain()
                E.wait(v, sA, f"{tag}_sq{i}")
                v.scalar_tensor_tensor(stat[:, 32 + c:33 + c],
                                       stat[:, 16 + c:17 + c], 1.0 / D,
                                       stat[:, 32 + c:33 + c], ALU.mult, ALU.add)
                v.drain()
                v.tensor_scalar(stat[:, 32 + c:33 + c], stat[:, 32 + c:33 + c],
                                LN_EPS, 1.0, ALU.add, ALU.mult)
                v.drain()
                v.reciprocal(stat[:, 32 + c:33 + c], stat[:, 32 + c:33 + c])
                E.inc(v, sV)
                E.mark(f"{tag}_pre{i}", sV)
                E.wait(v, sA, f"{tag}_sqrt{i}")
                v.tensor_scalar(dst_ap_fn(i), src,
                                stat[:, 48 + c:49 + c], stat[:, 32 + c:33 + c],
                                ALU.subtract, ALU.mult)
                E.inc(v, sV)
                E.mark(f"{tag}_norm{i}", sV)
                if xr_copy_from is not None and i < MT:
                    v.tensor_copy(XR[:, i, :], src)
                    E.inc(v, sV)
                    E.mark(f"{tag}_xr{i}", sV)

    @sec("vector")
    def _(v):
        v.memset(ones_bf[:], 1.0)
        E.inc(v, sV)
        E.mark("ones", sV)

    @sec("sync")
    def _(s):
        s.dma_start(out=cmsk[:], in_=cmask.rearrange("j p q -> p j q")).then_inc(dS.h, 16)
        dS.n += 16
        E.mark("cmask", dS)

    @sec("gpsimd")
    def _(g):
        for i, bsrc in enumerate([bq1, bk1, bv1, bo1]):
            g.dma_start(out=bias1[:, i, :], in_=bsrc).then_inc(dG.h, 16)
            dG.n += 16
        for wsrc, wint, wful, mk in [(wk0m, wk0i, wk0f, "coll1_k"),
                                     (wv0m, wv0i, wv0f, "coll1_v"),
                                     (wq0m, wq0i, wq0f, "coll1_q"),
                                     (wo0m, wo0i, wo0f, "coll1")]:
            src2 = wsrc.rearrange("(t p) d -> p t d", p=P)
            for j in range(2):
                g.dma_start(out=wtx[:, j, :, :],
                            in_=src2[:, j, :]
                            .rearrange("p (t q) -> p t q", t=NT)).then_inc(dG.h, 16)
                dG.n += 16
            g.wait_ge(dG.h, dG.n)
            g.dma_start(out=wint.rearrange("(t p) d -> p t d", p=P),
                        in_=wtx[:].rearrange("p a t q -> p a (t q)")).then_inc(dG.h, 16)
            dG.n += 16
            g.wait_ge(dG.h, dG.n)
            cc = g.collective_compute(
                "AllGather", ALU.bypass,
                replica_groups=[list(range(8))],
                ins=[wint], outs=[wful])
            E.inc(cc, sG)
            E.mark(mk, sG)
        E.mark("w0cast", dG)

    @sec("sync")
    def _(s):
        for t in range(NT):
            if t >= 1:
                E.wait(s, sV, f"ln0_done{t - 1}")
            s.dma_start(out=AF[:], in_=x0t[t]).then_inc(dA.h, 16)
            dA.n += 16
            E.mark(f"x0_{t}", dA)

    @sec("scalar")
    def _(s):
        for t in range(NT):
            E.wait(s, dA, f"x0_{t}")
            s.activation(tbuf3[:], AF[:, 0:1024], ACTF.Square,
                         accum_out=stat[:, 16:17])
            s.activation(tbuf3[:], AF[:, 1024:D], ACTF.Square,
                         accum_out=stat[:, 17:18])
            E.inc(s, sA)
            E.mark(f"ln0_sq{t}", sA)
            E.wait(s, sV, f"ln0_pre{t}")
            s.activation(stat[:, 32:33], stat[:, 32:33], ACTF.Sqrt)
            E.inc(s, sA)
            E.mark(f"ln0_sqrt{t}", sA)

    @sec("vector")
    def _(v):
        for t in range(NT):
            E.wait(v, dA, f"x0_{t}")
            v.reduce_sum(stat[:, 0:1], AF[:], AX.X)
            v.drain()
            v.tensor_scalar(stat[:, 48:49], stat[:, 0:1], 1.0 / D, 0.0,
                            ALU.mult, ALU.add)
            v.drain()
            v.scalar_tensor_tensor(stat[:, 32:33], stat[:, 48:49], -1.0,
                                   stat[:, 48:49], ALU.mult, ALU.mult)
            v.drain()
            E.wait(v, sA, f"ln0_sq{t}")
            v.scalar_tensor_tensor(stat[:, 32:33], stat[:, 16:17], 1.0 / D,
                                   stat[:, 32:33], ALU.mult, ALU.add)
            v.drain()
            v.scalar_tensor_tensor(stat[:, 32:33], stat[:, 17:18], 1.0 / D,
                                   stat[:, 32:33], ALU.mult, ALU.add)
            v.drain()
            v.tensor_scalar(stat[:, 32:33], stat[:, 32:33], LN_EPS, 1.0,
                            ALU.add, ALU.mult)
            v.drain()
            v.reciprocal(stat[:, 32:33], stat[:, 32:33])
            E.inc(v, sV)
            E.mark(f"ln0_pre{t}", sV)
            E.wait(v, sA, f"ln0_sqrt{t}")
            v.tensor_scalar(lnx_tm[:, t, :], AF[:], stat[:, 48:49],
                            stat[:, 32:33], ALU.subtract, ALU.mult)
            E.inc(v, sV)
            if t < MT:
                v.tensor_copy(XR[:, t, :], AF[:])
                E.inc(v, sV)
            E.mark(f"ln0_done{t}", sV)
        E.mark("ln0_all", sV)

    @sec("sync")
    def _(s):
        E.wait(s, sV, "ln0_all")
        for t in range(NT):
            s.dma_start(out=lnx_fm[:, :, t * P:(t + 1) * P],
                        in_=lnx_tm[:, t, :],
                        transpose=True).then_inc(dS.h, 16)
            dS.n += 16
        E.mark("lnxfm", dS)

    def emit_tern_store(s, j):
        kind, idx, cidx = tern_jobs[j]
        E.wait(s, sV, f"tj{j}_done")
        s.wait_ge(dA.h, dA.n)
        if kind == "tern_fc":
            s.dma_start(out=vsl[:],
                        in_=tbuf2[:].rearrange("p (t q) -> p t q", t=NT),
                        transpose=True).then_inc(dA.h, 16)
            dA.n += 16
            s.wait_ge(dA.h, dA.n)
            s.dma_start(out=w1ti.rearrange("(t p) q -> p t q", p=P)
                        [:, :, idx * P:(idx + 1) * P],
                        in_=vsl[:]).then_inc(dA.h, 16)
            dA.n += 16
        elif kind == "tern_pr":
            s.dma_start(out=vsl[:],
                        in_=tbuf2[:].rearrange("p (t q) -> p t q", t=NT),
                        transpose=True).then_inc(dA.h, 16)
            dA.n += 16
            s.wait_ge(dA.h, dA.n)
            s.dma_start(out=w2ti.rearrange("(c t p) q -> p c t q", p=P, t=NT)
                        [:, cidx, :, idx * P:(idx + 1) * P],
                        in_=vsl[:]).then_inc(dA.h, 16)
            dA.n += 16
        else:
            s.dma_start(out=wd1i.rearrange("(t p) d -> p t d", p=P)[:, idx, :],
                        in_=tbuf2[:]).then_inc(dA.h, 16)
            dA.n += 16
        E.mark(f"tj{j}_stored", dA)

    @sec("sync")
    def _(s):
        E.wait(s, sG, "coll1_k")
        E.wait(s, dG, "w0cast")
        E.wait(s, sV, "ln0_all")
        E.wait(s, dS, "lnxfm")
        for h in range(NH0 := 16):
            E.wait(s, sP, f"h{h - 1}_kproj") if h > 0 else None
            if h > 0:
                pass
            s.dma_start(out=wtx[:, 0, :, :],
                        in_=wk0f_t[:, h, :].rearrange("p (t q) -> p t q", t=NT),
                        transpose=True).then_inc(dS.h, 16)
            dS.n += 16
            E.mark(f"h{h}_wk", dS)
            E.wait(s, sP, f"h{h - 1}_vproj") if h > 0 else None
            if h == 0:
                E.wait(s, sG, "coll1_v")
            s.dma_start(out=wtx[:, 1, :, :],
                        in_=wv0f_t[:, h, :].rearrange("p (t q) -> p t q", t=NT),
                        transpose=True).then_inc(dS.h, 16)
            dS.n += 16
            E.mark(f"h{h}_wv", dS)
            if h == 0:
                E.wait(s, sG, "coll1_q")
            E.wait(s, sP, f"h{h}_kproj")
            s.dma_start(out=wtx[:, 0, :, :],
                        in_=wq0f_t[:, h, :].rearrange("p (t q) -> p t q", t=NT),
                        transpose=True).then_inc(dS.h, 16)
            dS.n += 16
            E.mark(f"h{h}_wq", dS)
            E.wait(s, sA, f"h{h}_vevac3")
            s.dma_start(out=vtm0[:],
                        in_=vfm0.rearrange("p (t q) -> p t q", t=NT),
                        transpose=True).then_inc(dS.h, 16)
            dS.n += 16
            E.mark(f"h{h}_vtm", dS)
            if h >= 1:
                emit_tern_store(s, h - 1)
        for j in range(15, len(tern_jobs)):
            emit_tern_store(s, j)

    @sec("tensor")
    def _(te):
        for h in range(16):
            E.wait(te, dS, f"h{h}_wk")
            if h > 0:
                E.wait(te, sA, f"h{h - 1}_exp15")
                E.wait(te, sV, f"h{h - 1}_atevac")
            for c in range(4):
                ps, off = PS4[c]
                if h > 0:
                    E.wait(te, sA, f"h{h - 1}_qevac1")
                for dt in range(NT):
                    te.matmul(ps[:, off:off + 512], wtx[:, 0, dt, :],
                              lnx_fm[:, dt, c * 512:(c + 1) * 512],
                              start=(dt == 0), stop=(dt == NT - 1))
            E.inc(te.engine_nop() if False else te.nop(), sP) if False else None
            E.inc(te, sP)
            E.mark(f"h{h}_kproj", sP)
            E.wait(te, dS, f"h{h}_wv")
            for c in range(4):
                ps, off = PS4[c]
                E.wait(te, sA, f"h{h}_kevac{c}")
                for dt in range(NT):
                    te.matmul(ps[:, off:off + 512], wtx[:, 1, dt, :],
                              lnx_fm[:, dt, c * 512:(c + 1) * 512],
                              start=(dt == 0), stop=(dt == NT - 1))
            E.inc(te, sP)
            E.mark(f"h{h}_vproj", sP)
            E.wait(te, dS, f"h{h}_wq")
            for c in range(2):
                ps, off = PS4[c]
                E.wait(te, sA, f"h{h}_vevac{c}")
                for dt in range(NT):
                    te.matmul(ps[:, off:off + 512], wtx[:, 0, dt, :],
                              lnx_fm[:, dt, c * 512:(c + 1) * 512],
                              start=(dt == 0), stop=(dt == NT - 1))
            E.inc(te, sP)
            E.mark(f"h{h}_qproj", sP)
            E.wait(te, dS, f"h{h}_vtm")
            E.wait(te, dS, "cmask")
            for kp in range(16):
                qoff, pieces = SC_PIECES[kp]
                scps = psA if (kp % 2 == 0) else psB
                if kp >= 2:
                    E.wait(te, sA, f"h{h}_exp{kp - 2}")
                else:
                    E.wait(te, sA, f"h{h}_qevac1")
                for (c0, c1) in pieces:
                    te.matmul(scps[:, c0:c1], kh0[:, kp * P:(kp + 1) * P],
                              qh0[:, c0:c1], start=True, stop=True)
                E.inc(te, sP)
                E.mark(f"h{h}_sc{kp}", sP)
                E.wait(te, sV, f"h{h}_mask{kp}")
                for (c0, c1) in pieces:
                    te.matmul(psC[:, c0:c1], vtm0[:, kp, :],
                              pt0[:, kp % 2, c0:c1],
                              start=(kp == 0), stop=(kp == 15),
                              skip_group_check=True)
                for (c0, c1) in pieces:
                    te.matmul(psD[:, c0:c1], ones_bf[:], pt0[:, kp % 2, c0:c1],
                              start=(kp == 0), stop=(kp == 15),
                              skip_group_check=True)
                E.inc(te, sP)
                E.mark(f"h{h}_av{kp}", sP)

    @sec("scalar")
    def _(s):
        for h in range(16):
            for c in range(4):
                E.wait(s, sP, f"h{h}_kproj")
                ps, off = PS4[c]
                s.activation(kh0[:, c * 512:(c + 1) * 512], ps[:, off:off + 512],
                             ACTF.Copy)
                E.inc(s, sA)
                E.mark(f"h{h}_kevac{c}", sA)
            for c in range(4):
                E.wait(s, sP, f"h{h}_vproj")
                ps, off = PS4[c]
                s.activation(vfm0[:, c * 512:(c + 1) * 512], ps[:, off:off + 512],
                             ACTF.Copy)
                E.inc(s, sA)
                E.mark(f"h{h}_vevac{c}", sA)
            for c in range(2):
                E.wait(s, sP, f"h{h}_qproj")
                ps, off = PS4[c]
                s.activation(qh0[:, c * 512:(c + 1) * 512], ps[:, off:off + 512],
                             ACTF.Copy)
                E.inc(s, sA)
                E.mark(f"h{h}_qevac{c}", sA)
            for kp in range(16):
                qoff, pieces = SC_PIECES[kp]
                scps = psA if (kp % 2 == 0) else psB
                E.wait(s, sP, f"h{h}_sc{kp}")
                if kp >= 2:
                    E.wait(s, sP, f"h{h}_av{kp - 2}")
                s.activation(pt0[:, kp % 2, qoff:1024], scps[:, qoff:1024],
                             ACTF.Exp, scale=INV_SQRT_DK0)
                E.inc(s, sA)
                E.mark(f"h{h}_exp{kp}", sA)

    def emit_tern_dve(v, j):
        kind, idx, cidx = tern_jobs[j]
        E.wait(v, dG, f"tj{j}_load")
        if j >= 1:
            E.wait(v, dA, f"tj{j - 1}_stored")
        if kind in ("tern_fc", "tern_pr"):
            wb3 = tbuf[:].rearrange("p (g e) -> p g e", e=64)
            G = D // 64
            v.reduce_sum(gsc[:, 0:G], wb3, AX.X, apply_absolute_value=True)
            v.drain()
            v.tensor_scalar(gscb[:, 0:G], gsc[:, 0:G], 1.0 / 64, 0.0,
                            ALU.mult, ALU.add)
            v.drain()
            v.tensor_scalar(gscb[:, 0:G], gscb[:, 0:G], BF16_1EM8, 0.0,
                            ALU.max, ALU.add)
            v.drain()
            v.tensor_scalar(gsc2[:, 0:G], gscb[:, 0:G], TAU_F, 0.0,
                            ALU.mult, ALU.add)
            v.drain()
            v.tensor_scalar(gsc[:, 0:G], gscb[:, 0:G], -TAU_F, 0.0,
                            ALU.mult, ALU.add)
            v.drain()
            taub = gsc2[:, 0:G].unsqueeze(2).broadcast_to([P, G, 64])
            ntaub = gsc[:, 0:G].unsqueeze(2).broadcast_to([P, G, 64])
            scb = gscb[:, 0:G].unsqueeze(2).broadcast_to([P, G, 64])
            p3 = tbuf2[:].rearrange("p (g e) -> p g e", e=64)
            v.tensor_tensor(p3, wb3, taub, ALU.is_gt)
            v.drain()
            v.tensor_tensor(p3, p3, scb, ALU.mult)
            v.drain()
            v.tensor_tensor(wb3, wb3, ntaub, ALU.is_lt)
            v.drain()
            v.tensor_tensor(wb3, wb3, scb, ALU.mult)
            v.drain()
            v.tensor_tensor(p3, p3, wb3, ALU.subtract)
            E.inc(v, sV)
        else:
            G = D // P
            af3 = AF[:].rearrange("p (g e) -> p g e", e=P)
            v.reduce_max(gsc[:, 0:G], af3, AX.X, apply_absolute_value=True)
            v.drain()
            v.tensor_scalar(gsc2[:, 0:G], gsc[:, 0:G], 1.0 / 448.0, 0.0,
                            ALU.mult, ALU.add)
            v.drain()
            v.reciprocal(gsc[:, 0:G], gsc2[:, 0:G])
            v.drain()
            v.tensor_scalar(gsc[:, 0:G], gsc[:, 0:G], 0.5, 0.0,
                            ALU.mult, ALU.add)
            v.drain()
            r2b = gsc[:, 0:G].unsqueeze(2).broadcast_to([P, G, P])
            sb_ = gsc2[:, 0:G].unsqueeze(2).broadcast_to([P, G, P])
            f83 = f8b[:].rearrange("p (g e) -> p g e", e=P)
            q3 = tbuf2[:].rearrange("p (g e) -> p g e", e=P)
            v.tensor_tensor(f83, af3, r2b, ALU.mult)
            v.drain()
            v.scalar_tensor_tensor(q3, f83, 2.0, sb_, ALU.mult, ALU.mult)
            E.inc(v, sV)
        E.mark(f"tj{j}_done", sV)

    @sec("vector")
    def _(v):
        for h in range(16):
            for kp in range(16):
                qoff, _ = SC_PIECES[kp]
                E.wait(v, sA, f"h{h}_exp{kp}")
                v.tensor_tensor(pt0[:, kp % 2, qoff:qoff + P],
                                pt0[:, kp % 2, qoff:qoff + P],
                                cmsk[:, kp, :], ALU.mult)
                E.inc(v, sV)
                E.mark(f"h{h}_mask{kp}", sV)
            E.wait(v, sP, f"h{h}_av15")
            v.reciprocal(rec[:], psD[:])
            v.drain()
            v.tensor_tensor(attn0[:, h, :], psC[:], rec[:], ALU.mult)
            E.inc(v, sV)
            E.mark(f"h{h}_atevac", sV)
            if h < len(tern_jobs):
                emit_tern_dve(v, h)
        for j in range(16, len(tern_jobs)):
            emit_tern_dve(v, j)

    @sec("gpsimd")
    def _(g):
        for j, (kind, idx, cidx) in enumerate(tern_jobs):
            if j >= 1:
                E.wait(g, sV, f"tj{j - 1}_done")
            if kind == "tern_fc":
                src = wfcm.rearrange("(t p) d -> p t d", p=P)[:, idx, :]
                g.dma_start(out=tbuf[:], in_=src).then_inc(dG.h, 16)
            elif kind == "tern_pr":
                src = wprm.rearrange("(t p) d -> p t d", p=P)[:, idx,
                                                              cidx * D:(cidx + 1) * D]
                g.dma_start(out=tbuf[:], in_=src).then_inc(dG.h, 16)
            else:
                src = wf1m.rearrange("(t p) d -> p t d", p=P)[:, idx, :]
                E.wait(g, sV, "ln0_all")
                g.dma_start(out=AF[:], in_=src).then_inc(dG.h, 16)
            dG.n += 16
            E.mark(f"tj{j}_load", dG)
        for ci, co, lastj, mk in [(w1ti, w1tf, 7, "coll2a"),
                                  (w2ti, w2tf, 15, "coll2b"),
                                  (wd1i, wd1f, len(tern_jobs) - 1, "coll2")]:
            E.wait(g, dA, f"tj{lastj}_stored")
            cc = g.collective_compute(
                "AllGather", ALU.bypass,
                replica_groups=[list(range(8))],
                ins=[ci], outs=[co])
            E.inc(cc, sG)
            E.mark(mk, sG)

    @sec("sync")
    def _(s):
        for dc in range(4):
            E.wait(s, sV, "h15_atevac")
            if dc == 0:
                E.wait(s, sG, "coll1")
            if dc > 0:
                E.wait(s, sP, f"oproj_mm{dc - 1}")
            for i in range(4):
                s.dma_start(out=wdc[:, dc % 2, :, i * P:(i + 1) * P],
                            in_=wo0f_t[:, dc * 4 + i, :]
                            .rearrange("p (t q) -> p t q", t=NT),
                            transpose=True).then_inc(dS.h, 16)
                dS.n += 16
            E.mark(f"oproj_w{dc}", dS)

    @sec("tensor")
    def _(te):
        E.wait(te, sA, "h15_exp15")
        E.wait(te, sV, "h15_atevac")
        for dc in range(4):
            E.wait(te, dS, f"oproj_w{dc}")
            for qt in range(MT):
                ps, off = PS4[qt % 4]
                if dc > 0 or qt >= 4:
                    pj = (dc * 8 + qt) - 4
                    E.wait(te, sV, f"oproj_ev{pj}")
                for dt in range(NT):
                    te.matmul(ps[:, off:off + 512], attn0[:, dt, qt * P:(qt + 1) * P],
                              wdc[:, dc % 2, dt, :],
                              start=(dt == 0), stop=(dt == NT - 1))
                E.inc(te, sP)
                E.mark(f"oproj_mm{dc}_{qt}", sP)
            E.mark(f"oproj_mm{dc}", sP)

    @sec("vector")
    def _(v):
        for dc in range(4):
            for qt in range(MT):
                ps, off = PS4[qt % 4]
                E.wait(v, sP, f"oproj_mm{dc}_{qt}")
                v.tensor_tensor(XR[:, qt, dc * 512:(dc + 1) * 512],
                                ps[:, off:off + 512],
                                XR[:, qt, dc * 512:(dc + 1) * 512], ALU.add)
                E.inc(v, sV)
                E.mark(f"oproj_ev{dc * 8 + qt}", sV)
        E.mark("oproj_done", sV)

    for half in range(2):
        @sec("scalar")
        def _(s, half=half):
            for i in range(4):
                t = half * 4 + i
                E.wait(s, sV, f"ln1_{half}_src{i}")
                s.activation(tbuf3[:], XR[:, t, 0:1024], ACTF.Square,
                             accum_out=stat[:, 16:17])
                s.activation(tbuf3[:], XR[:, t, 1024:D], ACTF.Square,
                             accum_out=stat[:, 17:18])
                E.inc(s, sA)
                E.mark(f"ln1_{half}_sq{i}", sA)
                E.wait(s, sV, f"ln1_{half}_pre{i}")
                s.activation(stat[:, 32:33], stat[:, 32:33], ACTF.Sqrt)
                E.inc(s, sA)
                E.mark(f"ln1_{half}_sqrt{i}", sA)
                E.wait(s, sV, f"ln1_{half}_norm{i}")
                if half == 0 and i == 0:
                    E.wait(s, sG, "coll2")
                s.dma_start(out=ln1fm[:, :, i * P:(i + 1) * P],
                            in_=tbuf2[:].rearrange("p (t q) -> p t q", t=NT),
                            transpose=True).then_inc(dA.h, 16)
                dA.n += 16
                E.mark(f"ln1T_{half}_{i}", dA)
                E.mark(f"ln1T_{half}_m1_{i + 1}", dA)
            E.mark(f"ln1T_{half}", dA)

        @sec("vector")
        def _(v, half=half):
            for i in range(4):
                t = half * 4 + i
                if half == 0 and i == 0:
                    E.wait(v, sV, "oproj_done")
                if half == 1:
                    E.wait(v, sV, f"ffn_{0}_done")
                E.wait(v, dA, f"ln1T_{half}_{i - 1}") if i > 0 else None
                if i > 0:
                    pass
                v.reduce_sum(stat[:, 0:1], XR[:, t, :], AX.X)
                E.inc(v, sV)
                E.mark(f"ln1_{half}_src{i}", sV)
                v.drain()
                v.tensor_scalar(stat[:, 48:49], stat[:, 0:1], 1.0 / D, 0.0,
                                ALU.mult, ALU.add)
                v.drain()
                v.scalar_tensor_tensor(stat[:, 32:33], stat[:, 48:49], -1.0,
                                       stat[:, 48:49], ALU.mult, ALU.mult)
                v.drain()
                E.wait(v, sA, f"ln1_{half}_sq{i}")
                v.scalar_tensor_tensor(stat[:, 32:33], stat[:, 16:17], 1.0 / D,
                                       stat[:, 32:33], ALU.mult, ALU.add)
                v.drain()
                v.scalar_tensor_tensor(stat[:, 32:33], stat[:, 17:18], 1.0 / D,
                                       stat[:, 32:33], ALU.mult, ALU.add)
                v.drain()
                v.tensor_scalar(stat[:, 32:33], stat[:, 32:33], LN_EPS, 1.0,
                                ALU.add, ALU.mult)
                v.drain()
                v.reciprocal(stat[:, 32:33], stat[:, 32:33])
                E.inc(v, sV)
                E.mark(f"ln1_{half}_pre{i}", sV)
                E.wait(v, sA, f"ln1_{half}_sqrt{i}")
                if i > 0:
                    E.wait(v, dA, f"ln1T_{half}_m1_{i}")
                v.tensor_scalar(tbuf2[:], XR[:, t, :], stat[:, 48:49],
                                stat[:, 32:33], ALU.subtract, ALU.mult)
                E.inc(v, sV)
                E.mark(f"ln1_{half}_norm{i}", sV)

        @sec("sync")
        def _(s, half=half):
            for hc in range(16):
                E.wait(s, sG, "coll2")
                if half == 0 and hc == 0:
                    E.wait(s, sP, "oproj_mm3")
                if half == 1 and hc < 2:
                    E.wait(s, sV, "ffn_0_done")
                if hc >= 2:
                    E.wait(s, sP, f"mm1_{half}_{hc - 2}")
                r, co = hc // 2, (hc % 2) * 512
                s.dma_start(out=w1c[:, hc % 2, :, :],
                            in_=w1tf_t[:, r, :, co:co + 512]).then_inc(dA.h, 16)
                dA.n += 16
                E.mark(f"w1c_{half}_{hc}", dA)

        @sec("tensor")
        def _(te, half=half):
            def msmm(ht2):
                E.wait(te, sV, f"sq_{half}_{ht2}")
                te.matmul(psD[:, 0:512], ones_bf[:], sqb[:],
                          start=(ht2 == 0), stop=(ht2 == 63),
                          skip_group_check=True)
                E.inc(te, sP)
                E.mark(f"msmm_{half}_{ht2}", sP)

            E.wait(te, dA, f"ln1T_{half}")
            for hc in range(16):
                E.wait(te, dA, f"w1c_{half}_{hc}")
                for hsub in range(4):
                    ht = hc * 4 + hsub
                    ps, off = PS4[ht % 4]
                    if ht >= 4:
                        E.wait(te, sV, f"h2_{half}_{ht - 4}")
                    for dt in range(NT):
                        te.matmul(ps[:, off:off + 512],
                                  w1c[:, hc % 2, dt, hsub * P:(hsub + 1) * P],
                                  ln1fm[:, dt, :],
                                  start=(dt == 0), stop=(dt == NT - 1))
                    E.inc(te, sP)
                    E.mark(f"mm1_{half}_{hc}_{hsub}", sP)
                    if ht >= 4:
                        msmm(ht - 4)
                E.mark(f"mm1_{half}_{hc}", sP)
            for ht2 in range(60, 64):
                msmm(ht2)
            for dc in range(4):
                E.wait(te, dA, f"w2c_{half}_{dc}")
                for tt in range(4):
                    ps, off = PS4[(dc * 4 + tt) % 4]
                    pj = dc * 4 + tt - 4
                    if pj >= 0:
                        E.wait(te, sV, f"mm2ev_{half}_{pj}")
                    E.wait(te, sV, f"hn_{half}")
                    for ht in range(64):
                        te.matmul(ps[:, off:off + 512],
                                  hfm[:, ht, tt * P:(tt + 1) * P],
                                  w2c[:, ht, :],
                                  start=(ht == 0), stop=(ht == 63))
                    E.inc(te, sP)
                    E.mark(f"mm2_{half}_{dc}_{tt}", sP)

        @sec("scalar")
        def _(s, half=half):
            for hc in range(16):
                for hsub in range(4):
                    ht = hc * 4 + hsub
                    ps, off = PS4[ht % 4]
                    E.wait(s, sP, f"mm1_{half}_{hc}_{hsub}")
                    if ht >= 1:
                        E.wait(s, sV, f"h2_{half}_{ht - 1}")
                    s.activation(fstg[:], ps[:, off:off + 512], ACTF.Relu)
                    E.inc(s, sA)
                    E.mark(f"relu_{half}_{ht}", sA)
            E.wait(s, sV, f"msprep_{half}")
            s.activation(rstdb[:], rstdb[:], ACTF.Sqrt)
            E.inc(s, sA)
            E.mark(f"mssqrt_{half}", sA)

        @sec("vector")
        def _(v, half=half):
            for ht in range(64):
                E.wait(v, sA, f"relu_{half}_{ht}")
                v.tensor_tensor(hfm[:, ht, :], fstg[:], fstg[:], ALU.mult)
                E.inc(v, sV)
                E.mark(f"h2_{half}_{ht}", sV)
                v.drain()
                if ht >= 1:
                    E.wait(v, sP, f"msmm_{half}_{ht - 1}")
                v.tensor_tensor(sqb[:], hfm[:, ht, :], hfm[:, ht, :], ALU.mult)
                E.inc(v, sV)
                E.mark(f"sq_{half}_{ht}", sV)
            E.wait(v, sP, f"msmm_{half}_63")
            v.tensor_scalar(rstdb[:], psD[:, 0:512], 1.0 / HID, RMS_EPS,
                            ALU.mult, ALU.add)
            E.inc(v, sV)
            E.mark(f"msprep_{half}", sV)
            E.wait(v, sA, f"mssqrt_{half}")
            v.reciprocal(rstdb[:], rstdb[:])
            E.inc(v, sV)
            v.drain()
            for ht in range(64):
                v.tensor_tensor(hfm[:, ht, :], hfm[:, ht, :], rstdb[:], ALU.mult)
                v.drain()
            E.inc(v, sV)
            E.mark(f"hn_{half}", sV)
            for dc in range(4):
                for tt in range(4):
                    ps, off = PS4[(dc * 4 + tt) % 4]
                    E.wait(v, sP, f"mm2_{half}_{dc}_{tt}")
                    t = half * 4 + tt
                    v.tensor_tensor(XR[:, t, dc * 512:(dc + 1) * 512],
                                    ps[:, off:off + 512],
                                    XR[:, t, dc * 512:(dc + 1) * 512], ALU.add)
                    E.inc(v, sV)
                    E.mark(f"mm2ev_{half}_{dc * 4 + tt}", sV)
            E.mark(f"ffn_{half}_done", sV)

        @sec("scalar")
        def _(s, half=half):
            for dc in range(4):
                E.wait(s, sV, f"hn_{half}")
                if dc == 0:
                    E.wait(s, sG, "coll2")
                if dc > 0:
                    E.wait(s, sP, f"mm2_{half}_{dc - 1}_3")
                for r2 in range(2):
                    s.dma_start(out=w2c[:].rearrange("p t (r q) -> p r t q", r=2)
                                [:, r2, :, :],
                                in_=w2tf_t[:, 2 * dc + r2, :, :]).then_inc(dA.h, 16)
                    dA.n += 16
                E.mark(f"w2c_{half}_{dc}", dA)

    @sec("scalar")
    def _(s):
        for i in range(MT):
            E.wait(s, sV, f"ln2_src{i}")
            s.activation(tbuf3[:], XR[:, i, 0:1024], ACTF.Square,
                         accum_out=stat[:, 16:17])
            s.activation(tbuf3[:], XR[:, i, 1024:D], ACTF.Square,
                         accum_out=stat[:, 17:18])
            E.inc(s, sA)
            E.mark(f"ln2_sq{i}", sA)
            E.wait(s, sV, f"ln2_pre{i}")
            s.activation(stat[:, 32:33], stat[:, 32:33], ACTF.Sqrt)
            E.inc(s, sA)
            E.mark(f"ln2_sqrt{i}", sA)

    @sec("vector")
    def _(v):
        E.wait(v, sV, "ffn_1_done")
        for i in range(MT):
            v.reduce_sum(stat[:, 0:1], XR[:, i, :], AX.X)
            E.inc(v, sV)
            E.mark(f"ln2_src{i}", sV)
            v.drain()
            v.tensor_scalar(stat[:, 48:49], stat[:, 0:1], 1.0 / D, 0.0,
                            ALU.mult, ALU.add)
            v.drain()
            v.scalar_tensor_tensor(stat[:, 32:33], stat[:, 48:49], -1.0,
                                   stat[:, 48:49], ALU.mult, ALU.mult)
            v.drain()
            E.wait(v, sA, f"ln2_sq{i}")
            v.scalar_tensor_tensor(stat[:, 32:33], stat[:, 16:17], 1.0 / D,
                                   stat[:, 32:33], ALU.mult, ALU.add)
            v.drain()
            v.scalar_tensor_tensor(stat[:, 32:33], stat[:, 17:18], 1.0 / D,
                                   stat[:, 32:33], ALU.mult, ALU.add)
            v.drain()
            v.tensor_scalar(stat[:, 32:33], stat[:, 32:33], LN_EPS, 1.0,
                            ALU.add, ALU.mult)
            v.drain()
            v.reciprocal(stat[:, 32:33], stat[:, 32:33])
            E.inc(v, sV)
            E.mark(f"ln2_pre{i}", sV)
            E.wait(v, sA, f"ln2_sqrt{i}")
            v.tensor_scalar(ln2tm[:, i, :], XR[:, i, :], stat[:, 48:49],
                            stat[:, 32:33], ALU.subtract, ALU.mult)
            E.inc(v, sV)
            E.mark(f"ln2_norm{i}", sV)
        E.mark("ln2_all", sV)

    @sec("sync")
    def _(s):
        E.wait(s, sV, "ln2_all")
        for t in range(MT):
            s.dma_start(out=ln2mf[:, :, t * P:(t + 1) * P],
                        in_=ln2tm[:, t, :],
                        transpose=True).then_inc(dS.h, 16)
            dS.n += 16
        E.mark("ln2mfT", dS)

    @sec("gpsimd")
    def _(g):
        E.wait(g, dS, "ln2mfT")
        g.dma_start(out=ln2i.rearrange("(t p) q -> p t q", p=P),
                    in_=ln2mf[:]).then_inc(dG.h, 16)
        dG.n += 16
        g.wait_ge(dG.h, dG.n)
        cc = g.collective_compute(
            "AllGather", ALU.bypass,
            replica_groups=[[0, 1], [2, 3], [4, 5], [6, 7]],
            ins=[ln2i], outs=[ln2f])
        E.inc(cc, sG)
        E.mark("coll3", sG)

    @sec("sync")
    def _(s):
        for h in range(2):
            for hdt in range(8):
                j = h * 8 + hdt
                E.wait(s, dS, "ln2mfT")
                if j >= 2:
                    E.wait(s, sP, f"q1mm_{j - 2}")
                s.dma_start(out=wtx[:, j % 2, :, :],
                            in_=wq1_t[:, j, :].rearrange("p (t q) -> p t q", t=NT),
                            transpose=True).then_inc(dS.h, 16)
                dS.n += 16
                E.mark(f"wq1T_{j}", dS)

    @sec("tensor")
    def _(te):
        for h in range(2):
            for hdt in range(8):
                j = h * 8 + hdt
                E.wait(te, dS, f"wq1T_{j}")
                for c in range(2):
                    ps, off = PS4[(2 * j + c) % 4]
                    pj = 2 * j + c - 4
                    if pj >= 0:
                        E.wait(te, sA, f"q1ev_{pj}")
                    for dt in range(NT):
                        te.matmul(ps[:, off:off + 512], wtx[:, j % 2, dt, :],
                                  ln2mf[:, dt, c * 512:(c + 1) * 512],
                                  start=(dt == 0), stop=(dt == NT - 1))
                    E.inc(te, sP)
                    E.mark(f"q1mmc_{2 * j + c}", sP)
                E.mark(f"q1mm_{j}", sP)

    @sec("scalar")
    def _(s):
        for h in range(2):
            for hdt in range(8):
                j = h * 8 + hdt
                for c in range(2):
                    ps, off = PS4[(2 * j + c) % 4]
                    E.wait(s, sP, f"q1mmc_{2 * j + c}")
                    s.activation(q01[:, h, hdt, c * 512:(c + 1) * 512],
                                 ps[:, off:off + 512], ACTF.Relu,
                                 bias=bias1[:, 0, j:j + 1])
                    E.inc(s, sA)
                    E.mark(f"q1ev_{2 * j + c}", sA)
        E.mark("q1done", sA)

    @sec("scalar")
    def _(s):
        E.wait(s, sG, "coll3")
        E.wait(s, sV, "ln2_all")
        E.wait(s, dS, "ln2mfT")
        for c in range(4):
            r, co = c // 2, (c % 2) * 512
            s.dma_start(out=fmch4[:, c, :, :],
                        in_=ln2f_t[:, r, :, co:co + 512]).then_inc(dA.h, 16)
            dA.n += 16
        E.mark("fm4", dA)

    @sec("sync")
    def _(s):
        for dvt in range(16):
            E.wait(s, sA, "q1done")
            if dvt >= 2:
                E.wait(s, sP, f"v1mm_{dvt - 2}")
            s.dma_start(out=wtx[:, dvt % 2, :, :],
                        in_=wv1_t[:, dvt, :].rearrange("p (t q) -> p t q", t=NT),
                        transpose=True).then_inc(dS.h, 16)
            dS.n += 16
            E.mark(f"wv1T_{dvt}", dS)

    @sec("tensor")
    def _(te):
        E.wait(te, dA, "fm4")
        for dvt in range(16):
            E.wait(te, dS, f"wv1T_{dvt}")
            for c in range(4):
                ps, off = PS4[(4 * dvt + c) % 4]
                pj = 4 * dvt + c - 4
                if pj >= 0:
                    E.wait(te, sA, f"v1ev_{pj}")
                for dt in range(NT):
                    te.matmul(ps[:, off:off + 512], wtx[:, dvt % 2, dt, :],
                              fmch4[:, c, dt, :],
                              start=(dt == 0), stop=(dt == NT - 1))
                E.inc(te, sP)
                E.mark(f"v1mmc_{4 * dvt + c}", sP)
            E.mark(f"v1mm_{dvt}", sP)

    @sec("scalar")
    def _(s):
        for dvt in range(16):
            for c in range(4):
                ps, off = PS4[(4 * dvt + c) % 4]
                E.wait(s, sP, f"v1mmc_{4 * dvt + c}")
                if 4 * dvt + c > 0:
                    E.wait(s, dG, f"v1st_{4 * dvt + c - 1}")
                s.activation(vstg[:], ps[:, off:off + 512], ACTF.Relu,
                             bias=bias1[:, 2, dvt:dvt + 1])
                E.inc(s, sA)
                E.mark(f"v1ev_{4 * dvt + c}", sA)

    @sec("gpsimd")
    def _(g):
        for dvt in range(16):
            for c in range(4):
                E.wait(g, sA, f"v1ev_{4 * dvt + c}")
                g.dma_start(out=v1fm_t[:, dvt, c * 512:(c + 1) * 512],
                            in_=vstg[:]).then_inc(dG.h, 16)
                dG.n += 16
                E.mark(f"v1st_{4 * dvt + c}", dG)
        E.mark("v1done", dG)

    for h in range(2):
        @sec("gpsimd")
        def _(s, h=h):
            for kc in range(4):
                if h == 0:
                    break
                E.wait(s, sV, "h0_l1done")
                if kc >= 1:
                    E.wait(s, sP, f"k1mm_{h}_{kc - 1}")
                r, co = kc // 2, (kc % 2) * 512
                s.dma_start(out=fmch1[:, :, :],
                            in_=ln2f_t[:, r, :, co:co + 512]).then_inc(dA.h, 16)
                dA.n += 16
                E.mark(f"fm1_{h}_{kc}", dA)

        @sec("sync")
        def _(s, h=h):
            for kc in range(4):
                for hdt in range(8):
                    j2 = kc * 8 + hdt
                    E.wait(s, dG, "v1done")
                    if j2 >= 2:
                        E.wait(s, sP, f"k1mmc_{h}_{j2 - 2}")
                    s.dma_start(out=wtx[:, hdt % 2, :, :],
                                in_=wk1_t[:, h * 8 + hdt, :]
                                .rearrange("p (t q) -> p t q", t=NT),
                                transpose=True).then_inc(dS.h, 16)
                    dS.n += 16
                    E.mark(f"wk1T_{h}_{j2}", dS)

        @sec("tensor")
        def _(te, h=h):
            for kc in range(4):
                if h == 1:
                    E.wait(te, dA, f"fm1_{h}_{kc}")
                for hdt in range(8):
                    E.wait(te, dS, f"wk1T_{h}_{kc * 8 + hdt}")
                    ps, off = PS4[(kc * 8 + hdt) % 4]
                    pj = kc * 8 + hdt - 4
                    if pj >= 0:
                        E.wait(te, sA, f"k1ev_{h}_{pj}")
                    src = fmch4[:, kc, :, :] if h == 0 else fmch1[:, :, :]
                    for dt in range(NT):
                        te.matmul(ps[:, off:off + 512], wtx[:, hdt % 2, dt, :],
                                  src[:, dt, :],
                                  start=(dt == 0), stop=(dt == NT - 1))
                    E.inc(te, sP)
                    E.mark(f"k1mmc_{h}_{kc * 8 + hdt}", sP)
                    if hdt == 7:
                        E.mark(f"k1mm_{h}_{kc}", sP)
            for kp in range(16):
                scps = psA if (kp % 2 == 0) else psB
                if kp >= 2:
                    E.wait(te, sA, f"exp1_{h}_{kp - 2}")
                else:
                    E.wait(te, sA, f"k1ev_{h}_31")
                for (c0, c1) in FULL_PIECES:
                    for hdt in range(8):
                        te.matmul(scps[:, c0:c1], kh1[:, hdt, kp * P:(kp + 1) * P],
                                  q01[:, h, hdt, c0:c1],
                                  start=(hdt == 0), stop=(hdt == 7))
                E.inc(te, sP)
                E.mark(f"sc1_{h}_{kp}", sP)
                E.wait(te, sA, f"exp1_{h}_{kp}") if False else None
            for kp in range(16):
                E.wait(te, sA, f"exp1_{h}_{kp}")
                for (c0, c1) in FULL_PIECES:
                    te.matmul(psD[:, c0:c1], ones_bf[:], pt1[:, kp, c0:c1],
                              start=(kp == 0), stop=(kp == 15),
                              skip_group_check=True)
                E.inc(te, sP)
                E.mark(f"cs1_{h}_{kp}", sP)
            for dvt in range(8):
                avps = psC if (dvt % 2 == 0) else psD
                E.wait(te, dS, f"vsl_{h}_{dvt}")
                if dvt == 1:
                    E.wait(te, sV, f"rec1_{h}")
                if dvt >= 2:
                    E.wait(te, sV, f"av1ev_{h}_{dvt - 2}")
                for kp in range(16):
                    for (c0, c1) in FULL_PIECES:
                        te.matmul(avps[:, c0:c1], vsl[:, kp, :],
                                  pt1[:, kp, c0:c1],
                                  start=(kp == 0), stop=(kp == 15),
                                  skip_group_check=True)
                E.inc(te, sP)
                E.mark(f"av1_{h}_{dvt}", sP)

        @sec("scalar")
        def _(s, h=h):
            for kc in range(4):
                for hdt in range(8):
                    ps, off = PS4[(kc * 8 + hdt) % 4]
                    E.wait(s, sP, f"k1mmc_{h}_{kc * 8 + hdt}")
                    s.activation(kh1[:, hdt, kc * 512:(kc + 1) * 512],
                                 ps[:, off:off + 512], ACTF.Relu,
                                 bias=bias1[:, 1, h * 8 + hdt:h * 8 + hdt + 1])
                    E.inc(s, sA)
                    E.mark(f"k1ev_{h}_{kc * 8 + hdt}", sA)
            for kp in range(16):
                scps = psA if (kp % 2 == 0) else psB
                E.wait(s, sP, f"sc1_{h}_{kp}")
                if h == 0 and kp == 0:
                    E.wait(s, sV, "h0pt_free")
                s.activation(pt1[:, kp, :], scps[:], ACTF.Exp,
                             scale=INV_SQRT_HD1)
                E.inc(s, sA)
                E.mark(f"exp1_{h}_{kp}", sA)

        @sec("sync")
        def _(s, h=h):
            for dvt in range(8):
                E.wait(s, sP, f"cs1_{h}_15")
                if dvt >= 1:
                    E.wait(s, sP, f"av1_{h}_{dvt - 1}")
                s.dma_start(out=vsl[:],
                            in_=v1fm_t[:, h * 8 + dvt, :]
                            .rearrange("p (t q) -> p t q", t=NT),
                            transpose=True).then_inc(dS.h, 16)
                dS.n += 16
                E.mark(f"vsl_{h}_{dvt}", dS)

        @sec("vector")
        def _(v, h=h):
            if h == 0:
                E.wait(v, sV, "ffn_1_done")
                E.inc(v, sV)
                E.mark("h0pt_free", sV)
            E.wait(v, sP, f"cs1_{h}_15")
            v.reciprocal(rec[:], psD[:])
            E.inc(v, sV)
            E.mark(f"rec1_{h}", sV)
            for dvt in range(8):
                avps = psC if (dvt % 2 == 0) else psD
                E.wait(v, sP, f"av1_{h}_{dvt}")
                v.tensor_tensor(attn1[:, h * 8 + dvt, :], avps[:], rec[:],
                                ALU.mult)
                E.inc(v, sV)
                E.mark(f"av1ev_{h}_{dvt}", sV)
            E.inc(v, sV)
            E.mark(f"h{h}_l1done", sV)

    @sec("sync")
    def _(s):
        for dot in range(16):
            E.wait(s, sV, "h1_l1done")
            if dot >= 2:
                E.wait(s, sP, f"o1mm_{dot - 2}")
            s.dma_start(out=wtx[:, dot % 2, :, :],
                        in_=wo1_t[:, dot, :].rearrange("p (t q) -> p t q", t=NT),
                        transpose=True).then_inc(dS.h, 16)
            dS.n += 16
            E.mark(f"wo1T_{dot}", dS)
        for dt in range(NT):
            E.wait(s, sA, "o1evall")
            s.dma_start(out=o1tm[:, :, dt * P:(dt + 1) * P],
                        in_=o1fm[:, dt, :],
                        transpose=True).then_inc(dS.h, 16)
            dS.n += 16
        E.mark("o1tmT", dS)

    @sec("tensor")
    def _(te):
        for dot in range(16):
            E.wait(te, dS, f"wo1T_{dot}")
            for c in range(2):
                ps, off = PS4[(2 * dot + c) % 4]
                pj = 2 * dot + c - 4
                if pj >= 0:
                    E.wait(te, sA, f"o1ev_{pj}")
                for dt in range(NT):
                    te.matmul(ps[:, off:off + 512], wtx[:, dot % 2, dt, :],
                              attn1[:, dt, c * 512:(c + 1) * 512],
                              start=(dt == 0), stop=(dt == NT - 1))
                E.inc(te, sP)
                E.mark(f"o1mmc_{2 * dot + c}", sP)
            E.mark(f"o1mm_{dot}", sP)

    @sec("scalar")
    def _(s):
        for dot in range(16):
            for c in range(2):
                ps, off = PS4[(2 * dot + c) % 4]
                E.wait(s, sP, f"o1mmc_{2 * dot + c}")
                s.activation(o1fm[:, dot, c * 512:(c + 1) * 512],
                             ps[:, off:off + 512], ACTF.Relu,
                             bias=bias1[:, 3, dot:dot + 1])
                E.inc(s, sA)
                E.mark(f"o1ev_{2 * dot + c}", sA)
        E.mark("o1evall", sA)

    @sec("vector")
    def _(v):
        E.wait(v, dS, "o1tmT")
        for tt in range(MT):
            v.tensor_tensor(XR[:, tt, :], o1tm[:, tt, :], XR[:, tt, :], ALU.add)
            E.inc(v, sV)
        E.inc(v, sV)
        E.mark("l1res_done", sV)

    @sec("scalar")
    def _(s):
        for i in range(MT):
            E.wait(s, sV, f"ln3_src{i}")
            s.activation(tbuf3[:], XR[:, i, 0:1024], ACTF.Square,
                         accum_out=stat[:, 16:17])
            s.activation(tbuf3[:], XR[:, i, 1024:D], ACTF.Square,
                         accum_out=stat[:, 17:18])
            E.inc(s, sA)
            E.mark(f"ln3_sq{i}", sA)
            E.wait(s, sV, f"ln3_pre{i}")
            s.activation(stat[:, 32:33], stat[:, 32:33], ACTF.Sqrt)
            E.inc(s, sA)
            E.mark(f"ln3_sqrt{i}", sA)

    @sec("vector")
    def _(v):
        E.wait(v, sV, "l1res_done")
        for i in range(MT):
            v.reduce_sum(stat[:, 0:1], XR[:, i, :], AX.X)
            E.inc(v, sV)
            E.mark(f"ln3_src{i}", sV)
            v.drain()
            v.tensor_scalar(stat[:, 48:49], stat[:, 0:1], 1.0 / D, 0.0,
                            ALU.mult, ALU.add)
            v.drain()
            v.scalar_tensor_tensor(stat[:, 32:33], stat[:, 48:49], -1.0,
                                   stat[:, 48:49], ALU.mult, ALU.mult)
            v.drain()
            E.wait(v, sA, f"ln3_sq{i}")
            v.scalar_tensor_tensor(stat[:, 32:33], stat[:, 16:17], 1.0 / D,
                                   stat[:, 32:33], ALU.mult, ALU.add)
            v.drain()
            v.scalar_tensor_tensor(stat[:, 32:33], stat[:, 17:18], 1.0 / D,
                                   stat[:, 32:33], ALU.mult, ALU.add)
            v.drain()
            v.tensor_scalar(stat[:, 32:33], stat[:, 32:33], LN_EPS, 1.0,
                            ALU.add, ALU.mult)
            v.drain()
            v.reciprocal(stat[:, 32:33], stat[:, 32:33])
            E.inc(v, sV)
            E.mark(f"ln3_pre{i}", sV)
            E.wait(v, sA, f"ln3_sqrt{i}")
            v.tensor_scalar(AF[:], XR[:, i, :], stat[:, 48:49],
                            stat[:, 32:33], ALU.subtract, ALU.mult)
            E.inc(v, sV)
            v.drain()
            G = D // P
            af3 = AF[:].rearrange("p (g e) -> p g e", e=P)
            v.reduce_max(gsc[:, 0:G], af3, AX.X, apply_absolute_value=True)
            v.drain()
            v.tensor_scalar(gsc2[:, 0:G], gsc[:, 0:G], 1.0 / 448.0, 0.0,
                            ALU.mult, ALU.add)
            v.drain()
            v.reciprocal(gsc[:, 0:G], gsc2[:, 0:G])
            v.drain()
            v.tensor_scalar(gsc[:, 0:G], gsc[:, 0:G], 0.5, 0.0,
                            ALU.mult, ALU.add)
            v.drain()
            r2b = gsc[:, 0:G].unsqueeze(2).broadcast_to([P, G, P])
            sb_ = gsc2[:, 0:G].unsqueeze(2).broadcast_to([P, G, P])
            f83 = f8b[:].rearrange("p (g e) -> p g e", e=P)
            x3 = xdtm[:, i, :].rearrange("p (g e) -> p g e", e=P)
            v.tensor_tensor(f83, af3, r2b, ALU.mult)
            v.drain()
            v.scalar_tensor_tensor(x3, f83, 2.0, sb_, ALU.mult, ALU.mult)
            E.inc(v, sV)
            E.mark(f"xq_{i}", sV)
        E.mark("xq_all", sV)

    @sec("sync")
    def _(s):
        E.wait(s, sV, "xq_all")
        for t in range(MT):
            s.dma_start(out=xdfm[:, :, t * P:(t + 1) * P],
                        in_=xdtm[:, t, :],
                        transpose=True).then_inc(dS.h, 16)
            dS.n += 16
        E.mark("xdfmT", dS)
        for dc in range(4):
            E.wait(s, sG, "coll2")
            if dc >= 2:
                E.wait(s, sP, f"ffn1mm_{dc - 2}_7")
            for i in range(4):
                s.dma_start(out=wdc[:, dc % 2, :, i * P:(i + 1) * P],
                            in_=wd1f_t[:, dc * 4 + i, :]
                            .rearrange("p (t q) -> p t q", t=NT),
                            transpose=True).then_inc(dS.h, 16)
                dS.n += 16
            E.mark(f"wdcT_{dc}", dS)

    @sec("tensor")
    def _(te):
        E.wait(te, dS, "xdfmT")
        for dc in range(4):
            E.wait(te, dS, f"wdcT_{dc}")
            for qt in range(MT):
                ps, off = PS4[qt % 4]
                pj = dc * 8 + qt - 4
                if pj >= 0:
                    E.wait(te, sV, f"ffn1ev_{pj}")
                for dt in range(NT):
                    te.matmul(ps[:, off:off + 512], xdfm[:, dt, qt * P:(qt + 1) * P],
                              wdc[:, dc % 2, dt, :],
                              start=(dt == 0), stop=(dt == NT - 1))
                E.inc(te, sP)
                E.mark(f"ffn1mm_{dc}_{qt}", sP)

    @sec("vector")
    def _(v):
        for dc in range(4):
            for qt in range(MT):
                ps, off = PS4[qt % 4]
                E.wait(v, sP, f"ffn1mm_{dc}_{qt}")
                if dc * 8 + qt > 0:
                    E.wait(v, dG, f"ost_{dc * 8 + qt - 1}")
                v.tensor_tensor(fstg[:], ps[:, off:off + 512],
                                XR[:, qt, dc * 512:(dc + 1) * 512], ALU.add)
                E.inc(v, sV)
                E.mark(f"ffn1ev_{dc * 8 + qt}", sV)

    @sec("gpsimd")
    def _(g):
        for dc in range(4):
            for qt in range(MT):
                E.wait(g, sV, f"ffn1ev_{dc * 8 + qt}")
                g.dma_start(out=outt[qt][:, dc * 512:(dc + 1) * 512],
                            in_=fstg[:]).then_inc(dG.h, 16)
                dG.n += 16
                E.mark(f"ost_{dc * 8 + qt}", dG)
        g.wait_ge(dG.h, dG.n)

    NH0 = 16
    if _SEC_LIMIT is not None:
        del sections[_SEC_LIMIT:]
    if _DUMP is not None:
        _mark, _what = _DUMP
        _srcs = {"XR": XR[:], "lnx": lnx_tm[:, 0:MT, :],
                 "ab1": AB1[:].rearrange("p (t d) -> p t d", t=NT)[:, 0:MT, :],
                 "ab1hi": AB1[:].rearrange("p (t d) -> p t d", t=NT)[:, MT:NT, :],
                 "ab2": AB2[:].rearrange("p (t d) -> p t d", t=NT)[:, 0:MT, :],
                 "w1ti": w1ti.rearrange("(t a p) q -> p t a q", p=P, a=2),
                 "w2ti": w2ti.rearrange("(t a p) q -> p t a q", p=P, a=8)}
        _dsts = {"w1ti": out.rearrange("(t p) (a q) -> p t a q", p=P, q=1024),
                 "w2ti": out.rearrange("(t p) (a q) -> p t a q", p=P, q=256)}

        @sec("gpsimd")
        def _(g):
            E.wait(g, sV, _mark)
            if _what in _dsts:
                for _t in range(8):
                    g.dma_start(out=_dsts[_what][:, _t, :, :],
                                in_=_srcs[_what][:, _t, :, :]).then_inc(dG.h, 16)
                    dG.n += 16
            else:
                g.dma_start(out=out.rearrange("(t p) d -> p t d", p=P),
                            in_=_srcs[_what]).then_inc(dG.h, 16)
                dG.n += 16
            g.wait_ge(dG.h, dG.n)
    E.mock = True
    E.reset()
    mock = _MockEng()
    for engine, fn in sections:
        mock._engine = engine
        fn(mock)
    E.mock = False
    E.reset()
    mock = _MockEng()
    for engine, fn in sections:
        mock._engine = engine
        fn(mock)
    semname = {id(c.h): k for k, c in sems.items()}
    stuck, simsems = _simulate_queues(mock.trace)
    if stuck:
        byval = {}
        for mk, mv in E.marks.items():
            byval.setdefault(mv, []).append(mk)
        msg = []
        for e, (idx, it) in stuck.items():
            if it[0] == "wait":
                cands = [m for m in byval.get(it[2], [])]
                msg.append(f"{e}[{idx}]: wait {semname.get(it[1], it[1])}"
                           f" >= {it[2]} (cur {simsems.get(it[1], 0)})"
                           f" @lines {it[3]} marks~{cands[:6]}")
            else:
                msg.append(f"{e}[{idx}]: {it}")
        raise RuntimeError("queue deadlock detected:\n" + "\n".join(msg))
    E.mock = False
    E.reset()
    with nc.Block() as block:
        for engine, fn in sections:
            getattr(block, engine)(lambda eng, fn=fn: fn(_EngProxy(eng)))

    es.close()
    return nc


def _pack_inputs(inputs):
    f32 = np.float32
    bf16 = ml_dtypes.bfloat16
    x_all = np.ascontiguousarray(np.asarray(inputs["L0_attn_in_features"], dtype=f32))
    wq0 = np.asarray(inputs["L0_attn_q_proj"], dtype=f32)
    wk0 = np.asarray(inputs["L0_attn_k_proj"], dtype=f32)
    wv0 = np.asarray(inputs["L0_attn_v_proj"], dtype=f32)
    wo0 = np.asarray(inputs["L0_attn_o_proj"], dtype=f32)
    wfc = np.asarray(inputs["L0_ffn_W_fc"], dtype=f32)
    wpr = np.asarray(inputs["L0_ffn_W_proj"], dtype=f32)
    wf1 = np.asarray(inputs["L1_ffn_weight"], dtype=f32)

    def as_bf16(a):
        a = np.asarray(a)
        if a.dtype == np.uint16:
            return a.view(bf16)
        return a.astype(bf16)

    l1w = {k: as_bf16(inputs[k]) for k in
           ["L1_attn_Wq", "L1_attn_bq", "L1_attn_Wk", "L1_attn_bk",
            "L1_attn_Wv", "L1_attn_bv", "L1_attn_Wo", "L1_attn_bo"]}

    tri = np.tril(np.ones((P, P), dtype=np.float32)).T
    tri = tri.astype(bf16)
    zeros = np.zeros((P, P), dtype=bf16)
    ones = np.ones((P, P), dtype=bf16)

    in_maps = []
    for c in range(8):
        b, par = c // 2, c % 2
        xb = x_all[b].reshape(NT, P, D)
        order = list(range(par, NT, 2)) + list(range(1 - par, NT, 2))
        x_perm = np.ascontiguousarray(xb[order].reshape(NT * P, D))
        cm = np.empty((16, P, P), dtype=bf16)
        for kp in range(16):
            if kp < 8:
                cm[kp] = tri
            else:
                cm[kp] = zeros if par == 0 else ones
        r = c
        sl = slice(r * 256, (r + 1) * 256)
        in_maps.append({
            "x0": x_perm,
            "cmask": cm,
            "wq0m": np.ascontiguousarray(wq0[sl]),
            "wk0m": np.ascontiguousarray(wk0[sl]),
            "wv0m": np.ascontiguousarray(wv0[sl]),
            "wo0m": np.ascontiguousarray(wo0[sl]),
            "wfcm": np.ascontiguousarray(wfc[r * 1024:(r + 1) * 1024]),
            "wprm": np.ascontiguousarray(wpr[sl]),
            "wq1": l1w["L1_attn_Wq"], "wk1": l1w["L1_attn_Wk"],
            "wv1": l1w["L1_attn_Wv"], "wo1": l1w["L1_attn_Wo"],
            "bq1": np.ascontiguousarray(l1w["L1_attn_bq"].reshape(16, P).T),
            "bk1": np.ascontiguousarray(l1w["L1_attn_bk"].reshape(16, P).T),
            "bv1": np.ascontiguousarray(l1w["L1_attn_bv"].reshape(16, P).T),
            "bo1": np.ascontiguousarray(l1w["L1_attn_bo"].reshape(16, P).T),
            "wf1m": np.ascontiguousarray(wf1[sl]),
        })
    return in_maps


def kernel(**inputs):
    in_maps = _pack_inputs(inputs)
    nc = build_nc()
    res = run_bass_kernel_spmd(nc, in_maps, core_ids=list(range(8)))
    outp = np.empty((4, NT * P, D), dtype=np.float32)
    for c in range(8):
        b, par = c // 2, c % 2
        oc = np.asarray(res.results[c]["out"], dtype=np.float32).reshape(MT, P, D)
        for t in range(MT):
            g = 2 * t + par
            outp[b, g * P:(g + 1) * P, :] = oc[t]
    return outp


if __name__ == "__main__":
    nc = build_nc()
    print("build OK; instructions:",
          sum(1 for _ in nc.all_instructions()) if hasattr(nc, "all_instructions") else "?")
